# version 16
# baseline (speedup 1.0000x reference)
import math
import numpy as np
import concourse.bass as bass
import concourse.mybir as mybir
from concourse.bass_utils import run_bass_kernel_spmd

F32 = mybir.dt.float32
BF16 = mybir.dt.bfloat16
U8 = mybir.dt.uint8
AF = mybir.ActivationFunctionType
ALU = mybir.AluOpType

NCORES = 8
T = 4096
D = 1024
NDC = D // 128
DH = 4096
NHC = DH // 128
SCALE_A = float(D) ** -0.25
SCALE_B = 1.0 / math.sqrt(float(D))
LNSB = math.log(SCALE_B)
EPS = 1e-5
EXP_SHIFT = -20.0

NSUB = 4
BTRUNC = 4

ENGS = ("sp", "act", "dve", "pe", "pool")


class Prog:

    def __init__(self):
        self.ops = {e: [] for e in ENGS}
        self.cnt = {}
        self.pending = {e: {} for e in ENGS}

    def sem(self, name):
        assert name not in self.cnt
        self.cnt[name] = 0

    def op(self, eng, method, *args, waits=(), inc=None, tag="", **kwargs):
        w = dict(self.pending[eng])
        self.pending[eng] = {}
        for s, v in waits:
            assert v <= self.cnt[s], (tag, s, v, self.cnt[s])
            if w.get(s, -1) < v:
                w[s] = v
        post = None
        if inc is not None:
            s, n = inc
            self.cnt[s] += n
            post = self.cnt[s]
        self.ops[eng].append((tuple(w.items()), method, args, kwargs, inc, tag))
        return post

    def after(self, eng, waits):
        p = self.pending[eng]
        for s, v in waits:
            if p.get(s, -1) < v:
                p[s] = v

    def barrier(self, exclude=("cv",)):
        snap = {s: v for s, v in self.cnt.items() if v > 0 and s not in exclude}
        for e in ENGS:
            p = self.pending[e]
            for s, v in snap.items():
                if p.get(s, -1) < v:
                    p[s] = v

    def replay(self, eng, proxy, sems):
        for waits, method, args, kwargs, inc, tag in self.ops[eng]:
            for s, v in waits:
                proxy.wait_ge(sems[s], v)
            inst = getattr(proxy, method)(*args, **kwargs)
            if inc is not None:
                assert inst is not None, f"op {tag} returned no instruction"
                inst.then_inc(sems[inc[0]], inc[1])


class Arena:
    def __init__(self, nc, pbytes):
        self.h = nc.alloc_sbuf_tensor("arena", [128, pbytes], U8)
        self.pbytes = pbytes
        self._v = {}

    def view(self, off, npart, cols, dtype):
        isz = mybir.dt.size(dtype)
        assert off % 64 == 0 and off + cols * isz <= self.pbytes, (off, cols, isz)
        key = (off, npart, cols, str(dtype))
        if key not in self._v:
            self._v[key] = self.h.bitcast(dtype).ap()[0:npart, off // isz: off // isz + cols]
        return self._v[key]


class DBuf:

    def __init__(self, P, name, nslots=2):
        self.P, self.name, self.n = P, name, nslots
        self.ld = [f"{name}{i}" for i in range(nslots)]
        for s in self.ld:
            P.sem(s)
        self.issued = [0] * nslots
        self.i = 0

    def load(self, qeng, dst, src, waits=()):
        s = self.i % self.n
        self.i += 1
        post = self.P.op(qeng, "dma_start", out=dst, in_=src, waits=waits,
                         inc=(self.ld[s], 16), tag=self.name)
        self.issued[s] += 1
        assert post == 16 * self.issued[s]
        return (self.ld[s], post)


def build_program(nsub=NSUB):
    nc = bass.Bass("TRN2", target_bir_lowering=False)
    P = Prog()
    for s in ("pe", "act", "dve", "pool"):
        P.sem(s)

    xin = nc.dram_tensor("xT", [D, T], F32, kind="ExternalInput").ap()
    wqa = nc.dram_tensor("wqa", [128, 128], F32, kind="ExternalInput").ap()
    wka = nc.dram_tensor("wka", [128, 128], F32, kind="ExternalInput").ap()
    wva = nc.dram_tensor("wva", [128, 128], F32, kind="ExternalInput").ap()
    wut = nc.dram_tensor("wut", [D, D], F32, kind="ExternalInput").ap()
    but = nc.dram_tensor("but", [128, NDC], F32, kind="ExternalInput").ap()
    wqbt = nc.dram_tensor("wqbt", [NDC, 128, NDC * 128], F32, kind="ExternalInput").ap()
    wkbt = nc.dram_tensor("wkbt", [NDC, 128, NDC * 128], F32, kind="ExternalInput").ap()
    wvbt = nc.dram_tensor("wvbt", [D, D], F32, kind="ExternalInput").ap()
    wobt = nc.dram_tensor("wobt", [NDC, 128, NDC * 128], F32, kind="ExternalInput").ap()
    dlog = nc.dram_tensor("dlog", [1, 1], F32, kind="ExternalInput").ap()
    oscl = nc.dram_tensor("oscl", [1, 1], F32, kind="ExternalInput").ap()
    w1t = nc.dram_tensor("w1t", [NHC, 128, NDC * 128], F32, kind="ExternalInput").ap()
    w2t = nc.dram_tensor("w2t", [NHC, 128, NDC * 128], F32, kind="ExternalInput").ap()
    h1t = nc.dram_tensor("h1t", [NHC, 128, NHC * 128], F32, kind="ExternalInput").ap()
    h2t = nc.dram_tensor("h2t", [NHC, 128, NHC * 128], F32, kind="ExternalInput").ap()
    w3t = nc.dram_tensor("w3t", [NDC, 128, NHC * 128], F32, kind="ExternalInput").ap()
    s1v = nc.dram_tensor("s1v", [128, NHC], F32, kind="ExternalInput").ap()
    s2v = nc.dram_tensor("s2v", [128, NHC], F32, kind="ExternalInput").ap()
    sh1v = nc.dram_tensor("sh1v", [128, NHC], F32, kind="ExternalInput").ap()
    sh2v = nc.dram_tensor("sh2v", [128, NHC], F32, kind="ExternalInput").ap()
    s3v = nc.dram_tensor("s3v", [128, NDC], F32, kind="ExternalInput").ap()
    m1c = nc.dram_tensor("m1c", [128, 896], F32, kind="ExternalInput").ap()
    ioc = nc.dram_tensor("ioc", [128, 512], F32, kind="ExternalInput").ap()
    dm1 = nc.dram_tensor("dm1", [1, 32], F32, kind="ExternalInput").ap()
    on1 = nc.dram_tensor("on1", [1, 128], F32, kind="ExternalInput").ap()
    cstv = nc.dram_tensor("cstv", [128, 6], F32, kind="ExternalInput").ap()
    yout = nc.dram_tensor("y", [D, T], F32, kind="ExternalOutput").ap()

    xr1 = nc.dram_tensor("xr1", [D, T], F32, kind="Internal").ap()
    xr2 = nc.dram_tensor("xr2", [D, T], F32, kind="Internal").ap()
    xr3 = nc.dram_tensor("xr3", [D, T], F32, kind="Internal").ap()
    w1s = nc.dram_tensor("w1s", [NHC, 128, NDC * 128], BF16, kind="Internal").ap()
    w2s = nc.dram_tensor("w2s", [NHC, 128, NDC * 128], BF16, kind="Internal").ap()
    h1s = nc.dram_tensor("h1s", [NHC, 128, NHC * 128], BF16, kind="Internal").ap()
    h2s = nc.dram_tensor("h2s", [NHC, 128, NHC * 128], BF16, kind="Internal").ap()
    w3s = nc.dram_tensor("w3s", [NDC, 128, NHC * 128], BF16, kind="Internal").ap()
    wqbs = nc.dram_tensor("wqbs", [NDC, 128, NDC * 128], BF16, kind="Internal").ap()
    wkbs = nc.dram_tensor("wkbs", [NDC, 128, NDC * 128], BF16, kind="Internal").ap()
    wvbs = nc.dram_tensor("wvbs", [D, D], BF16, kind="Internal").ap()
    wobs = nc.dram_tensor("wobs", [NDC, 128, NDC * 128], BF16, kind="Internal").ap()
    qS = nc.dram_tensor("qS", [8, 128, NDC * 512], BF16, kind="Internal").ap()
    kS = nc.dram_tensor("kS", [8, 128, NDC * 512], BF16, kind="Internal").ap()
    vS = nc.dram_tensor("vS", [8, 128, 4 * 1024], BF16, kind="Internal").ap()

    def sbuf(name, p, cols, dtype):
        return nc.alloc_sbuf_tensor(name, [p, cols], dtype).ap()

    t1m = sbuf("t1m", 128, 896, F32)
    t0m = sbuf("t0m", 128, 896, BF16)
    cio = sbuf("cio", 128, 512, F32)
    ldt = sbuf("ldt", 128, 32, F32)
    lndb = sbuf("lndb", 128, 1, F32)
    oscb = sbuf("oscb", 128, 1, F32)
    onbf = sbuf("onbf", 128, 1, BF16)
    onf = sbuf("onf", 1, 128, F32)
    busb = sbuf("busb", 128, NDC, F32)
    s1b = sbuf("s1b", 128, NHC, F32)
    s2b = sbuf("s2b", 128, NHC, F32)
    sh1b = sbuf("sh1b", 128, NHC, F32)
    sh2b = sbuf("sh2b", 128, NHC, F32)
    s3b = sbuf("s3b", 128, NDC, F32)
    wqab = sbuf("wqab", 128, 128, BF16)
    wkab = sbuf("wkab", 128, 128, BF16)
    wvab = sbuf("wvab", 128, 128, BF16)
    sc0 = sbuf("sc0", 1, 8, F32)
    ones2 = sbuf("ones2", 2, 64, BF16)
    dm1s = sbuf("dm1s", 1, 32, F32)
    cstb = sbuf("cstb", 128, 6, F32)

    ARENA = 196 * 1024
    A = Arena(nc, ARENA)
    ps = [nc.alloc_psum_tensor(f"ps{i}", [128, 1024], F32).ap() for i in range(4)]

    def bank(i):
        return ps[i // 2][:, (i % 2) * 512:(i % 2) * 512 + 512]

    def pair(i):
        return ps[i]

    xl = DBuf(P, "xl", 2)
    xs = DBuf(P, "xs", 2)
    xq = DBuf(P, "xq", 2)
    wl = DBuf(P, "wl", 2)
    w2l = DBuf(P, "w2l", 2)
    kl = DBuf(P, "kl", 2)
    vl = DBuf(P, "vl", 2)
    ql = DBuf(P, "ql", 1)
    st = DBuf(P, "st", 2)
    qst = DBuf(P, "qst", 2)

    P.sem("c0")
    for dst_ap, src_ap in (
        (t1m[:, :], m1c[:, :]), (cio[:, :], ioc[:, :]), (onf[:, :], on1[:, :]),
        (busb[:, :], but[:, :]), (s1b[:, :], s1v[:, :]), (s2b[:, :], s2v[:, :]),
        (sh1b[:, :], sh1v[:, :]), (sh2b[:, :], sh2v[:, :]), (s3b[:, :], s3v[:, :]),
        (sc0[0:1, 0:1], dlog[:, :]), (sc0[0:1, 1:2], oscl[:, :]),
        (dm1s[0:1, :], dm1[:, :]), (cstb[:, :], cstv[:, :]),
    ):
        P.op("sp", "dma_start", out=dst_ap, in_=src_ap, inc=("c0", 16), tag="c0")
    CV = [("c0", P.cnt["c0"])]

    P.op("dve", "memset", onbf[:, :], 1.0, tag="onbf")
    P.op("dve", "memset", ones2[:, :], 1.0, tag="one2")
    P.op("dve", "tensor_scalar", t0m[:, :], t1m[:, :], -1.0, 1.0, ALU.mult, ALU.add,
         waits=CV, tag="t0m")
    a0 = P.op("act", "activation", out=sc0[0:1, 2:3], in_=sc0[0:1, 0:1],
              func=AF.Sigmoid, waits=CV, inc=("act", 1), tag="sig")
    d0 = P.op("dve", "tensor_scalar_max", sc0[0:1, 3:4], sc0[0:1, 2:3], 1e-38,
              waits=[("act", a0)], inc=("dve", 1), tag="sigm")
    a1 = P.op("act", "activation", out=sc0[0:1, 4:5], in_=sc0[0:1, 3:4], func=AF.Ln,
              waits=[("dve", d0)], inc=("act", 1), tag="lnd")
    p1 = P.op("pe", "matmul", bank(0)[:, 0:1], onf[0:1, :], sc0[0:1, 4:5],
              start=True, stop=True, waits=[("act", a1)], inc=("pe", 1), tag="bc1")
    p2 = P.op("pe", "matmul", bank(1)[:, 0:1], onf[0:1, :], sc0[0:1, 1:2],
              start=True, stop=True, waits=CV, inc=("pe", 1), tag="bc2")
    p3 = P.op("pe", "matmul", bank(2)[:, 0:32], onf[0:1, :], dm1s[0:1, :],
              start=True, stop=True, waits=CV, inc=("pe", 1), tag="bc3")
    P.op("dve", "tensor_copy", lndb[:, :], bank(0)[:, 0:1], waits=[("pe", p1)],
         tag="lndc")
    P.op("dve", "tensor_copy", oscb[:, :], bank(1)[:, 0:1], waits=[("pe", p2)],
         tag="oscc")
    d3 = P.op("dve", "tensor_scalar", ldt[:, :], bank(2)[:, 0:32], lndb[:, 0:1],
              LNSB, ALU.mult, ALU.add, waits=[("pe", p3)], inc=("dve", 1), tag="ldt")

    wu_sb = A.view(0, 128, NDC * 1024, BF16)
    P.barrier()

    P.sem("cv")
    cvm = {}

    def cast_dma(dst_ap, src_ap, tag):
        P.op("pool", "dma_start", out=dst_ap, in_=src_ap, inc=("cv", 16), tag=tag)

    cast_dma(wqab[:, :], wqa[:, :], "cvqa")
    cast_dma(wkab[:, :], wka[:, :], "cvka")
    cast_dma(wvab[:, :], wva[:, :], "cvva")
    cvm["small"] = P.cnt["cv"]
    for ic in range(NDC):
        cast_dma(wu_sb[:, ic * 1024:(ic + 1) * 1024],
                 wut[ic * 128:(ic + 1) * 128, :], "cvwu")
    cvm["wu"] = P.cnt["cv"]
    if nsub >= 2:
        for oc in range(NHC):
            cast_dma(w1s[oc, :, :], w1t[oc, :, :], "cvw1")
        cvm["w1s"] = P.cnt["cv"]
        for oc in range(NHC):
            cast_dma(w2s[oc, :, :], w2t[oc, :, :], "cvw2")
        cvm["w2s"] = P.cnt["cv"]
        for oc in range(NHC):
            cast_dma(h1s[oc, :, :], h1t[oc, :, :], "cvh1")
        cvm["h1s"] = P.cnt["cv"]
        for oc in range(NHC):
            cast_dma(h2s[oc, :, :], h2t[oc, :, :], "cvh2")
        cvm["h2s"] = P.cnt["cv"]
        for oc in range(NDC):
            cast_dma(w3s[oc, :, :], w3t[oc, :, :], "cvw3")
        cvm["w3s"] = P.cnt["cv"]
    if nsub >= 3:
        for oc in range(NDC):
            sl = slice(oc * 128, (oc + 1) * 128)
            cast_dma(wqbs[oc, :, :], wqbt[oc, :, :], "cvqb")
            cast_dma(wkbs[oc, :, :], wkbt[oc, :, :], "cvkb")
            cast_dma(wvbs[sl, :], wvbt[sl, :], "cvvb")
            cast_dma(wobs[oc, :, :], wobt[oc, :, :], "cvob")
        cvm["attnb"] = P.cnt["cv"]

    def ln_stats(src, t0, tcols, sm, sr, xfv, xbv, xqv, tev, rel):
        nh = tcols // 512
        assert nh in (1, 2)
        for dc in range(NDC):
            s = dc % 2
            lw = xs.load("pool", xfv[s], src[dc * 128:(dc + 1) * 128, t0:t0 + tcols],
                         waits=rel.get(("f", s), []))
            ap = P.op("act", "activation", out=xqv[s], in_=xfv[s], func=AF.Square,
                      waits=[lw] + rel.get(("q", s), []), inc=("act", 1), tag="lnsq")
            dp = P.op("dve", "tensor_copy", xbv[s], xfv[s],
                      waits=[lw] + rel.get(("b", s), []), inc=("dve", 1), tag="lncp")
            rel[("f", s)] = [("act", ap), ("dve", dp)]
            pb = pq = None
            for h in range(nh):
                c = slice(h * 512, (h + 1) * 512)
                pb = P.op("pe", "matmul", pair(2)[0:1, c], onbf[:, :], xbv[s][:, c],
                          start=(dc == 0), stop=(dc == NDC - 1), waits=[("dve", dp)],
                          inc=("pe", 1), tag="ln_sx")
                pq = P.op("pe", "matmul", pair(3)[0:1, c], onbf[:, :], xqv[s][:, c],
                          start=(dc == 0), stop=(dc == NDC - 1), waits=[("act", ap)],
                          inc=("pe", 1), tag="ln_sq")
            rel[("b", s)] = [("pe", pb)]
            rel[("q", s)] = [("pe", pq)]
        pw = [("pe", P.cnt["pe"])]
        dm = P.op("dve", "tensor_scalar_mul", sm, pair(2)[0:1, 0:tcols], 1.0 / D,
                  waits=pw, inc=("dve", 1), tag="ln_m")
        P.op("dve", "tensor_scalar_mul", tev[0:1, 0:tcols], pair(3)[0:1, 0:tcols],
             1.0 / D, tag="ln_e")
        a2 = P.op("act", "activation", out=tev[0:1, tcols:2 * tcols], in_=sm,
                  func=AF.Square, waits=pw + [("dve", dm)], inc=("act", 1),
                  tag="ln_ms")
        d3 = P.op("dve", "tensor_sub", tev[0:1, 0:tcols], tev[0:1, 0:tcols],
                  tev[0:1, tcols:2 * tcols], waits=[("act", a2)], inc=("dve", 1),
                  tag="ln_v")
        a3 = P.op("act", "activation", out=tev[0:1, tcols:2 * tcols],
                  in_=tev[0:1, 0:tcols], func=AF.Sqrt, bias=cstb[0:1, 0:1],
                  waits=[("dve", d3)], inc=("act", 1), tag="ln_s")
        d4 = P.op("dve", "reciprocal", sr, tev[0:1, tcols:2 * tcols],
                  waits=[("act", a3)], inc=("dve", 1), tag="ln_r")
        return d4

    def ln_bcast(sm, sr, mb, rb, tcols, ready):
        nh = tcols // 512
        last = None
        for h in range(nh):
            c = slice(h * 512, (h + 1) * 512)
            w = [("dve", ready)] + ([("dve", last)] if last else [])
            pm = P.op("pe", "matmul", bank(4), onf[0:1, :], sm[0:1, c],
                      start=True, stop=True, waits=w, inc=("pe", 1), tag="bcm")
            P.op("dve", "tensor_copy", mb[:, c], bank(4), waits=[("pe", pm)],
                 tag="bcmc")
            pr = P.op("pe", "matmul", bank(5), onf[0:1, :], sr[0:1, c],
                      start=True, stop=True, waits=[("dve", ready)], inc=("pe", 1),
                      tag="bcr")
            last = P.op("dve", "tensor_copy", rb[:, c], bank(5), waits=[("pe", pr)],
                        inc=("dve", 1), tag="bcrc")
        return last

    def emit_attn_a(src, dst):
        import collections
        mb = A.view(16 * 1024, 128, T, F32)
        rb = A.view(32 * 1024, 128, T, F32)
        yT = A.view(48 * 1024, 128, NDC * T, BF16)
        ln_dc = A.view(112 * 1024, 128, T, BF16)
        qT = A.view(120 * 1024, 128, T, BF16)
        kT = A.view(128 * 1024, 128, T, BF16)
        v2 = A.view(136 * 1024, 128, 32 * 192, BF16)
        v2r = v2.rearrange("p (a b) -> p a b", b=192)
        xfh = [A.view(148 * 1024 + s * 8 * 1024, 128, 2048, F32) for s in range(2)]
        prb = [A.view(169 * 1024 + s * 2048, 128, 1024, BF16) for s in range(4)]
        tmpf = A.view(183 * 1024, 128, 512, F32)
        rbsF = A.view(185 * 1024, 64, 1024, F32)
        tev = A.view(183 * 1024, 1, 1024, F32)
        sxq = [A.view(161 * 1024 + s * 1024, 128, 512, BF16) for s in range(4)]
        sxb = [A.view(165 * 1024 + s * 1024, 128, 512, BF16) for s in range(4)]
        sxf = [A.view(169 * 1024 + s * 2048, 128, 512, F32) for s in range(4)]
        smcr = A.view(189 * 1024, 1, 1024, F32)
        smc = smcr[0:1, 0:512]
        src_c = smcr[0:1, 512:1024]

        lrel = {}
        smcr2 = [smc, src_c]

        def stats_steps(tc):
            t0s = tc * 512
            par = tc % 2
            bx, bq2 = bank(4 + 2 * par), bank(5 + 2 * par)
            stt = {}
            out = []

            def dc_step(dcs):
                def f():
                    s = dcs % 4
                    lw = xs.load("sp", sxf[s],
                                 src[dcs * 128:(dcs + 1) * 128, t0s:t0s + 512],
                                 waits=lrel.get(("f", s), []))
                    ap = P.op("act", "activation", out=sxq[s], in_=sxf[s],
                              func=AF.Square, waits=[lw] + lrel.get(("q", s), []),
                              inc=("act", 1), tag="lnsq")
                    dp = P.op("dve", "tensor_copy", sxb[s], sxf[s],
                              waits=[lw] + lrel.get(("b", s), []),
                              inc=("dve", 1), tag="lncp")
                    lrel[("f", s)] = [("act", ap), ("dve", dp)]
                    pb = P.op("pe", "matmul", bx[0:1, :], onbf[:, :], sxb[s],
                              start=(dcs == 0), stop=(dcs == NDC - 1),
                              waits=[("dve", dp)], inc=("pe", 1), tag="ln_sx")
                    pq = P.op("pe", "matmul", bq2[0:1, :], onbf[:, :], sxq[s],
                              start=(dcs == 0), stop=(dcs == NDC - 1),
                              waits=[("act", ap)], inc=("pe", 1), tag="ln_sq")
                    lrel[("b", s)] = [("pe", pb)]
                    lrel[("q", s)] = [("pe", pq)]
                    if dcs == NDC - 1:
                        stt["pelast"] = max(pb, pq)
                return f

            def math():
                pw = [("pe", stt["pelast"])] + lrel.get(("m", par), []) + \
                    lrel.get("bcg", [])
                dm = P.op("dve", "tensor_scalar_mul", smcr2[0][0:1, :], bx[0:1, :],
                          1.0 / D, waits=pw, inc=("dve", 1), tag="ln_m")
                P.op("dve", "tensor_scalar_mul", tev[0:1, 0:512], bq2[0:1, :],
                     1.0 / D, tag="ln_e")
                a2 = P.op("act", "activation", out=tev[0:1, 512:1024],
                          in_=smcr2[0][0:1, :], func=AF.Square,
                          waits=pw + [("dve", dm)], inc=("act", 1), tag="ln_ms")
                d3 = P.op("dve", "tensor_sub", tev[0:1, 0:512], tev[0:1, 0:512],
                          tev[0:1, 512:1024], waits=[("act", a2)], inc=("dve", 1),
                          tag="ln_v")
                a3 = P.op("act", "activation", out=tev[0:1, 512:1024],
                          in_=tev[0:1, 0:512], func=AF.Sqrt, bias=cstb[0:1, 0:1],
                          waits=[("dve", d3)], inc=("act", 1), tag="ln_s")
                d4 = P.op("dve", "reciprocal", smcr2[1][0:1, :], tev[0:1, 512:1024],
                          waits=[("act", a3)], inc=("dve", 1), tag="ln_r")
                stt["rdy"] = d4

            def bc():
                c = slice(t0s, t0s + 512)
                pm = P.op("pe", "matmul", bx, onf[0:1, :], smcr2[0][0:1, :],
                          start=True, stop=True, waits=[("dve", stt["rdy"])],
                          inc=("pe", 1), tag="bcm")
                P.op("dve", "tensor_copy", mb[:, c], bx, waits=[("pe", pm)],
                     tag="bcmc")
                pr = P.op("pe", "matmul", bq2, onf[0:1, :], smcr2[1][0:1, :],
                          start=True, stop=True, waits=[("dve", stt["rdy"])],
                          inc=("pe", 1), tag="bcr")
                last = P.op("dve", "tensor_copy", rb[:, c], bq2, waits=[("pe", pr)],
                            inc=("dve", 1), tag="bcrc")
                lrel[("m", par)] = [("dve", last)]
                lrel["bcg"] = [("pe", pr)]
            return [dc_step(d) for d in range(NDC)], math, bc

        prev = None
        for tc in range(8):
            steps, mth, bcf = stats_steps(tc)
            for j, f in enumerate(steps):
                f()
                if prev is not None:
                    if j == 1:
                        prev[0]()
                    elif j == 3:
                        prev[1]()
            prev = (mth, bcf)
        prev[0]()
        prev[1]()
        P.barrier()

        v2o = P.op("dve", "memset", v2r[:, :, 64:128], 1.0, inc=("dve", 1), tag="v2o")
        P.after("pe", [("dve", v2o)])

        xrel = {}
        ln_guard = [0]

        def emit_norm_chunk(dc, tc):
            h = tc // 4
            if tc % 4 == 0:
                w = xrel.get(h, []) + ([("pe", ln_guard[0])] if tc == 0 else [])
                xrel[("lw", h)] = xl.load(
                    "sp", xfh[h], src[dc * 128:(dc + 1) * 128,
                                      h * 2048:(h + 1) * 2048], waits=w)
            c = slice(tc * 512, tc * 512 + 512)
            cc = slice((tc % 4) * 512, (tc % 4) * 512 + 512)
            w = [xrel[("lw", h)]]
            if tc == 0:
                w.append(("pe", ln_guard[0]))
            P.op("dve", "tensor_sub", tmpf[:, :], xfh[h][:, cc], mb[:, c],
                 waits=w, tag="a_sb")
            post = P.op("dve", "tensor_mul", ln_dc[:, c], tmpf[:, :], rb[:, c],
                        inc=("dve", 1), tag="a_ml")
            xrel[h] = [("dve", post)]
            return post

        ln_done = None
        for tc in range(8):
            ln_done = emit_norm_chunk(0, tc)
        ln_ready = {0: ln_done}

        halfslot_rel = {}
        slot_rel = {}
        ybank_rel = {}
        abc_rel = []
        tail_prev = {}
        tailq = collections.deque()
        P.after("pe", [("cv", cvm["wu"])])

        def drain_tail(n):
            for _ in range(min(n, len(tailq))):
                tailq.popleft()()

        for dc in range(NDC):
            last_ln = ln_ready[dc]
            qkv_rel = {}
            item = 0
            qk_done = last_ln
            v_done = last_ln
            pv = None
            for tc in range(8):
                c = slice(tc * 512, tc * 512 + 512)
                r = item % 4
                item += 1
                pq = P.op("pe", "matmul", bank(r), wqab[:, :], ln_dc[:, c],
                          start=True, stop=True,
                          waits=[("dve", last_ln)] + qkv_rel.get(r, []),
                          inc=("pe", 1), tag="a_q")
                dq = P.op("dve", "tensor_copy", qT[:, c], bank(r),
                          waits=[("pe", pq)], inc=("dve", 1), tag="a_qc")
                qkv_rel[r] = [("dve", dq)]
                r = item % 4
                item += 1
                pk = P.op("pe", "matmul", bank(r), wkab[:, :], ln_dc[:, c],
                          start=True, stop=True, waits=qkv_rel.get(r, []),
                          inc=("pe", 1), tag="a_k")
                dk = P.op("dve", "tensor_copy", kT[:, c], bank(r), waits=[("pe", pk)],
                          inc=("dve", 1), tag="a_kc")
                qkv_rel[r] = [("dve", dk)]
                qk_done = dk
                r = item % 4
                item += 1
                wv = qkv_rel.get(r, [])
                for t4 in range(4):
                    tb = tc * 4 + t4
                    pv = P.op("pe", "matmul", bank(r)[:, t4 * 128:t4 * 128 + 128],
                              ln_dc[:, tb * 128:tb * 128 + 128], wvab[:, :],
                              start=True, stop=True, waits=wv if t4 == 0 else (),
                              inc=("pe", 1), tag="a_v")
                src_v = bank(r).rearrange("p (a cc dd) -> p a cc dd", a=4, dd=64)
                P.op("dve", "tensor_copy",
                     v2r[:, tc * 4:tc * 4 + 4, 0:64], src_v[:, :, 0, :],
                     waits=[("pe", pv)], tag="a_vcA")
                dv = P.op("dve", "tensor_copy",
                          v2r[:, tc * 4:tc * 4 + 4, 128:192], src_v[:, :, 1, :],
                          inc=("dve", 1), tag="a_vcB")
                qkv_rel[r] = [("dve", dv)]
                v_done = dv
                drain_tail(2)
            drain_tail(8)
            ln_guard[0] = pv

            for tc in range(8):
                nkb = 4 * tc + 4
                qc = slice(tc * 512, tc * 512 + 512)
                yb = ps[2 + tc % 2]
                seq = list(range(4 * tc, nkb)) + list(range(0, 4 * tc))
                probs = {}
                ypost = {}
                sA = {}
                sB = {}

                def emit_scores(j, kb):
                    kc = slice(kb * 128, kb * 128 + 128)
                    sa, sb_ = (2 * j) % 4, (2 * j + 1) % 4
                    wv = [("dve", max(qk_done, v_done))] + halfslot_rel.get(sa, [])
                    sA[j] = P.op("pe", "matmul", bank(sa), kT[0:64, kc],
                                 qT[0:64, qc], start=True, stop=True, waits=wv,
                                 inc=("pe", 1), tag="a_sA")
                    sB[j] = P.op("pe", "matmul", bank(sb_), kT[64:128, kc],
                                 qT[64:128, qc], start=True, stop=True,
                                 waits=halfslot_rel.get(sb_, []), inc=("pe", 1),
                                 tag="a_sB")

                def emit_probs(j, kb):
                    slot = prb[j % 4]
                    pp = ps[j % 2]
                    aa = P.op("act", "activation", out=slot[:, :], in_=pp[:, :],
                              func=AF.Exp, scale=cstb[:, 3:4], bias=cstb[:, 1:2],
                              waits=[("pe", sB[j])] + slot_rel.get(j % 4, []),
                              inc=("act", 1), tag="a_ex")
                    halfslot_rel[(2 * j) % 4] = [("act", aa)]
                    halfslot_rel[(2 * j + 1) % 4] = [("act", aa)]
                    if kb >= 4 * tc:
                        c0 = 384 - (kb * 128 - tc * 512)
                        P.op("dve", "tensor_mul", slot[:, 0:512], slot[:, 0:512],
                             t0m[:, c0:c0 + 512], waits=[("act", aa)], tag="a_mA")
                        mm = P.op("dve", "tensor_mul", slot[:, 512:1024],
                                  slot[:, 512:1024], t0m[:, c0:c0 + 512],
                                  inc=("dve", 1), tag="a_mB")
                        probs[j] = ([("dve", mm)], [("dve", mm)])
                    else:
                        probs[j] = ([("act", aa)], [("act", aa)])

                def emit_y(j, kb):
                    slot = prb[j % 4]
                    wA = list(probs[j][0])
                    wB = list(probs[j][1])
                    if j == 0:
                        wA = wA + ybank_rel.get(tc % 2, [])
                    P.op("pe", "matmul", yb[:, 0:512],
                         v2[:, kb * 192:kb * 192 + 128], slot[:, 0:512],
                         start=(j == 0), stop=(j == nkb - 1), waits=wA, tag="a_yA")
                    ypost[j] = P.op("pe", "matmul", yb[:, 512:1024],
                                    v2[:, kb * 192 + 64:kb * 192 + 192],
                                    slot[:, 512:1024], start=(j == 0),
                                    stop=(j == nkb - 1), waits=wB, inc=("pe", 1),
                                    tag="a_yB")
                    slot_rel[j % 4] = [("pe", ypost[j])]

                for j, kb in enumerate(seq):
                    emit_scores(j, kb)
                    emit_probs(j, kb)
                    drain_tail(2)
                    if j == 2:
                        if dc + 1 < NDC:
                            p = emit_norm_chunk(dc + 1, tc)
                            if tc == 7:
                                ln_ready[dc + 1] = p
                    if j >= 3:
                        emit_y(j - 3, seq[j - 3])
                for j in (nkb - 3, nkb - 2, nkb - 1):
                    emit_y(j, seq[j])

                def make_tail(dc=dc, tc=tc, qc=qc, yb=yb, ystop=ypost[nkb - 1]):
                    st_ = {}

                    def f_rcp():
                        P.op("dve", "reciprocal", rbsF[0:64, 0:512],
                             yb[64:128, 0:512],
                             waits=[("pe", ystop)] + abc_rel, tag="a_rcA")
                        st_["rcp"] = P.op("dve", "reciprocal",
                                          rbsF[0:64, 512:1024],
                                          yb[0:64, 512:1024], inc=("dve", 1),
                                          tag="a_rcB")

                    def f_dtyA():
                        st_["dtyA"] = P.op("dve", "tensor_mul",
                                           yT[:, dc * T:dc * T + T][0:64, qc],
                                           yb[0:64, 0:512], rbsF[0:64, 0:512],
                                           inc=("dve", 1), tag="a_yeA")

                    def f_dtyB():
                        post = P.op("dve", "tensor_mul",
                                    yT[:, dc * T:dc * T + T][64:128, qc],
                                    yb[64:128, 512:1024], rbsF[0:64, 512:1024],
                                    inc=("dve", 1), tag="a_yeB")
                        abc_rel[:] = [("dve", post)]
                        ybank_rel[tc % 2] = [("dve", st_["dtyA"]), ("dve", post)]

                    return [f_rcp, f_dtyA, f_dtyB]

                tailq.extend(make_tail())
                drain_tail(2)
        drain_tail(99)
        atail_dve = P.cnt["dve"]

        P.barrier()
        otf = [A.view(145 * 1024 + s * 2048, 128, 512, F32) for s in range(4)]
        xrf = [A.view(153 * 1024 + s * 2048, 128, 512, F32) for s in range(4)]
        xw = DBuf(P, "xw", 4)
        st4 = DBuf(P, "st4", 4)
        ev = {}
        ot_rel = {}
        xr_rel = {}
        marks_a = {}
        i = 0
        for tc in range(7, -1, -1):
            for dcO in range(NDC):
                qc = slice(tc * 512, tc * 512 + 512)
                s = i % 4
                bnk = bank(i % 4)
                p1 = None
                for ic in range(NDC):
                    p1 = P.op("pe", "matmul", bnk,
                              wu_sb[:, ic * 1024 + dcO * 128: ic * 1024 + dcO * 128 + 128],
                              yT[:, ic * T:ic * T + T][:, qc],
                              start=(ic == 0), stop=(ic == NDC - 1),
                              waits=ev.get(i % 4, []) if ic == 0 else (),
                              inc=("pe", 1) if ic == NDC - 1 else None, tag="a_wu")
                lw = xw.load("sp", xrf[s], src[dcO * 128:(dcO + 1) * 128, qc],
                             waits=xr_rel.get(s, []))
                dp = P.op("dve", "scalar_tensor_tensor", otf[s][:, :], bnk,
                          busb[:, dcO:dcO + 1], xrf[s][:, :], ALU.add, ALU.add,
                          waits=[("pe", p1), lw] + ot_rel.get(s, []),
                          inc=("dve", 1), tag="a_e")
                ev[i % 4] = [("dve", dp)]
                sw = st4.load("sp", dst[dcO * 128:(dcO + 1) * 128, qc],
                              otf[s][:, :], waits=[("dve", dp)])
                ot_rel[s] = [sw]
                xr_rel[s] = [("dve", dp)]
                i += 1
            if tc % 2 == 0:
                marks_a[tc // 2] = [(ld, P.cnt[ld]) for ld in st4.ld
                                    if P.cnt[ld] > 0]
        return {"marks": marks_a, "pre_dve": atail_dve,
                "end_dve": P.cnt["dve"]}

    def emit_ffn(src, dst, io=None):
        import collections
        a_sb = A.view(0, 128, NHC * 1024, BF16)
        g1 = A.view(64 * 1024, 128, NHC * 1024, BF16)
        lnv = A.view(128 * 1024, 128, NDC * 1024, BF16)
        smx = A.view(128 * 1024, 1, 1024, F32)
        srx = A.view(132 * 1024, 1, 1024, F32)
        tev = A.view(136 * 1024, 1, 2048, F32)
        mbb = A.view(144 * 1024, 128, 1024, F32)
        rbb = A.view(148 * 1024, 128, 1024, F32)
        w1sl = [A.view(152 * 1024 + s * 2048, 128, 1024, BF16) for s in range(2)]
        w2sl = [A.view(156 * 1024 + s * 2048, 128, 1024, BF16) for s in range(2)]
        hsl = [A.view(160 * 1024 + s * 8192, 128, 4096, BF16) for s in range(2)]
        siv = [A.view(176 * 1024 + s * 1024, 128, 512, BF16) for s in range(2)]
        xfv = [A.view(178 * 1024 + s * 4096, 128, 1024, F32) for s in range(2)]
        sxb = [A.view(186 * 1024 + s * 2048, 128, 1024, BF16) for s in range(2)]
        sxqv = A.view(190 * 1024, 128, 1024, BF16)
        otf = [A.view(192 * 1024 + s * 2048, 128, 512, F32) for s in range(2)]

        bg = collections.deque()
        io = io or {}
        in_marks = io.get("marks", {})
        w_gate = [("dve", io["end_dve"])] if "end_dve" in io else []
        entry_pe = P.cnt["pe"]
        if "pre_dve" in io:
            P.after("pe", [("dve", io["pre_dve"])])
        if "entry_pe_gate" in io:
            w_gate = w_gate + [("pe", io["entry_pe_gate"])]

        def drain(n):
            for _ in range(min(n, len(bg))):
                bg.popleft()()

        lrel = {}
        xshare = {}
        ln_ready = {}
        bank_rel = {}

        def queue_stats(tc):
            t0 = tc * 1024
            st = {}

            def dc_step(dc):
                def f():
                    s = dc % 2
                    lw = xs.load("pool", xfv[s],
                                 src[dc * 128:(dc + 1) * 128, t0:t0 + 1024],
                                 waits=in_marks.get(tc, []) + xshare.get(s, []))
                    ap = P.op("act", "activation", out=sxqv, in_=xfv[s],
                              func=AF.Square, waits=[lw] + lrel.get("qq", []),
                              inc=("act", 1), tag="lnsq")
                    dp = P.op("dve", "tensor_copy", sxb[s], xfv[s],
                              waits=[lw] + lrel.get(("b", s), []),
                              inc=("dve", 1), tag="lncp")
                    xshare[s] = [("act", ap), ("dve", dp)]
                    pb = pq = None
                    for h in range(2):
                        c = slice(h * 512, (h + 1) * 512)
                        pb = P.op("pe", "matmul", pair(2)[0:1, c], onbf[:, :],
                                  sxb[s][:, c], start=(dc == 0), stop=(dc == NDC - 1),
                                  waits=[("dve", dp)], inc=("pe", 1), tag="ln_sx")
                        pq = P.op("pe", "matmul", pair(3)[0:1, c], onbf[:, :],
                                  sxqv[:, c], start=(dc == 0), stop=(dc == NDC - 1),
                                  waits=[("act", ap)], inc=("pe", 1), tag="ln_sq")
                    lrel[("b", s)] = [("pe", pb)]
                    lrel["qq"] = [("pe", pq)]
                    if dc == NDC - 1:
                        st["pelast"] = max(pb, pq)
                return f

            def math():
                pw = [("pe", st["pelast"])]
                dm = P.op("dve", "tensor_scalar_mul", smx[0:1, :], pair(2)[0:1, 0:1024],
                          1.0 / D, waits=pw, inc=("dve", 1), tag="ln_m")
                P.op("dve", "tensor_scalar_mul", tev[0:1, 0:1024],
                     pair(3)[0:1, 0:1024], 1.0 / D, tag="ln_e")
                a2 = P.op("act", "activation", out=tev[0:1, 1024:2048],
                          in_=smx[0:1, :], func=AF.Square,
                          waits=pw + [("dve", dm)], inc=("act", 1), tag="ln_ms")
                d3 = P.op("dve", "tensor_sub", tev[0:1, 0:1024], tev[0:1, 0:1024],
                          tev[0:1, 1024:2048], waits=[("act", a2)], inc=("dve", 1),
                          tag="ln_v")
                a3 = P.op("act", "activation", out=tev[0:1, 1024:2048],
                          in_=tev[0:1, 0:1024], func=AF.Sqrt, bias=cstb[0:1, 0:1],
                          waits=[("dve", d3)], inc=("act", 1), tag="ln_s")
                d4 = P.op("dve", "reciprocal", srx[0:1, :], tev[0:1, 1024:2048],
                          waits=[("act", a3)], inc=("dve", 1), tag="ln_r")
                st["rdy"] = d4

            def bc(h):
                def f():
                    c = slice(h * 512, (h + 1) * 512)
                    pm = P.op("pe", "matmul", bank(4), onf[0:1, :], smx[0:1, c],
                              start=True, stop=True,
                              waits=[("dve", st["rdy"])] + st.get("bl", []),
                              inc=("pe", 1), tag="bcm")
                    P.op("dve", "tensor_copy", mbb[:, c], bank(4),
                         waits=[("pe", pm)], tag="bcmc")
                    pr = P.op("pe", "matmul", bank(5), onf[0:1, :], srx[0:1, c],
                              start=True, stop=True, waits=[("dve", st["rdy"])],
                              inc=("pe", 1), tag="bcr")
                    last = P.op("dve", "tensor_copy", rbb[:, c], bank(5),
                                waits=[("pe", pr)], inc=("dve", 1), tag="bcrc")
                    st["bl"] = [("dve", last)]
                    st["done"] = last
                return f

            return [dc_step(d) for d in range(NDC)] + [math, bc(0), bc(1)], st

        def queue_norm(tc, wfree, stt):
            t0 = tc * 1024

            def step(dc):
                def f():
                    s = dc % 2
                    lw = xl.load("pool", xfv[s],
                                 src[dc * 128:(dc + 1) * 128, t0:t0 + 1024],
                                 waits=[("dve", stt["done"])] + xshare.get(s, []))
                    P.op("dve", "tensor_sub", xfv[s][:, :], xfv[s][:, :], mbb[:, :],
                         waits=[lw] + ([("pe", wfree)] if dc == 0 else []), tag="f_sb")
                    post = P.op("dve", "tensor_mul", lnv[:, dc * 1024:(dc + 1) * 1024],
                                xfv[s][:, :], rbb[:, :], inc=("dve", 1), tag="f_ml")
                    xshare[s] = [("dve", post)]
                    if dc == NDC - 1:
                        ln_ready[tc] = post
                return f

            return [step(d) for d in range(NDC)]

        TCS = [3, 2, 1, 0]
        steps, stt = queue_stats(TCS[0])
        for f in steps:
            f()
        for f in queue_norm(TCS[0], 0, stt):
            f()

        marks_out = {}
        lnv_free = g1_free = None
        for ti, tc in enumerate(TCS):
            t0 = tc * 1024
            drain(99) if tc not in ln_ready else None
            last_ln = ln_ready[tc]
            wrel = {}
            p1 = p2 = None
            for oc in range(NHC):
                s = oc % 2
                lw1 = wl.load("sp", w1sl[s][:, :], w1s[oc, :, :],
                              waits=[("cv", cvm["w1s"])] + w_gate +
                              wrel.get(("1", s), []))
                lw2 = w2l.load("sp", w2sl[s][:, :], w2s[oc, :, :],
                               waits=[("cv", cvm["w2s"])] + w_gate +
                               wrel.get(("2", s), []))
                for hf in range(2):
                    j = oc * 2 + hf
                    b1i, b2i = j % 2, 2 + j % 2
                    b1, b2 = bank(b1i), bank(b2i)
                    for ic in range(NDC):
                        p1 = P.op("pe", "matmul", b1, w1sl[s][:, ic * 128:(ic + 1) * 128],
                                  lnv[:, ic * 1024 + hf * 512: ic * 1024 + hf * 512 + 512],
                                  start=(ic == 0), stop=(ic == NDC - 1),
                                  waits=[lw1, ("dve", last_ln)] + bank_rel.get(b1i, [])
                                  if ic == 0 else (),
                                  inc=("pe", 1) if ic == NDC - 1 else None, tag="f_w1")
                    sip = P.op("act", "activation", out=siv[hf][:, :], in_=b1,
                               func=AF.Silu, scale=s1b[:, oc:oc + 1],
                               waits=[("pe", p1)] + bank_rel.get(("si", hf), []),
                               inc=("act", 1), tag="f_si")
                    bank_rel[b1i] = [("act", sip)]
                    for ic in range(NDC):
                        p2 = P.op("pe", "matmul", b2, w2sl[s][:, ic * 128:(ic + 1) * 128],
                                  lnv[:, ic * 1024 + hf * 512: ic * 1024 + hf * 512 + 512],
                                  start=(ic == 0), stop=(ic == NDC - 1),
                                  waits=[lw2] + bank_rel.get(b2i, []) if ic == 0 else (),
                                  inc=("pe", 1) if ic == NDC - 1 else None, tag="f_w2")
                    gp = P.op("dve", "scalar_tensor_tensor",
                              a_sb[:, oc * 1024 + hf * 512: oc * 1024 + hf * 512 + 512],
                              b2, s2b[:, oc:oc + 1], siv[hf][:, :], ALU.mult, ALU.mult,
                              waits=[("pe", p2), ("act", sip)], inc=("dve", 1),
                              tag="f_ga")
                    bank_rel[b2i] = [("dve", gp)]
                    bank_rel[("si", hf)] = [("dve", gp)]
                wrel[("1", s)] = [("pe", p1)]
                wrel[("2", s)] = [("pe", p2)]
            a_done = ("dve", P.cnt["dve"])
            w2_free = p2
            if ti == len(TCS) - 1:
                lnv_free = P.cnt["pe"]

            if ti + 1 < len(TCS):
                steps, stt = queue_stats(TCS[ti + 1])
                bg.extend(steps)
                norm_steps = queue_norm(TCS[ti + 1], w2_free, stt)
            else:
                norm_steps = []

            def h_layer(wsrc, rsrc, wdst, sv, ready, cvk):
                wrel2 = {}
                gp = None
                p1 = None
                for oc in range(NHC):
                    s = oc % 2
                    lw1 = wl.load("sp", hsl[s][:, :], wsrc[oc, :, :],
                                  waits=[("cv", cvm[cvk])] +
                                  wrel2.get(s, [ready] if oc < 2 else []))
                    for hf in range(2):
                        j = oc * 2 + hf
                        bi = j % 4
                        b1 = bank(bi)
                        for ic in range(NHC):
                            p1 = P.op("pe", "matmul", b1,
                                      hsl[s][:, ic * 128:(ic + 1) * 128],
                                      rsrc[:, ic * 1024 + hf * 512: ic * 1024 + hf * 512 + 512],
                                      start=(ic == 0), stop=(ic == NHC - 1),
                                      waits=[lw1, ready] + bank_rel.get(bi, [])
                                      if ic == 0 else (),
                                      inc=("pe", 1) if ic == NHC - 1 else None,
                                      tag="f_h")
                        gp = P.op("act", "activation",
                                  out=wdst[:, oc * 1024 + hf * 512: oc * 1024 + hf * 512 + 512],
                                  in_=b1, func=AF.Gelu, scale=sv[:, oc:oc + 1],
                                  waits=[("pe", p1)], inc=("act", 1), tag="f_ge")
                        bank_rel[bi] = [("act", gp)]
                    wrel2[s] = [("pe", p1)]
                    drain(1)
                return ("act", gp)

            g1_done = h_layer(h1s, a_sb, g1, sh1b, a_done, "h1s")
            bg.extend(norm_steps)
            g2_done = h_layer(h2s, g1, a_sb, sh2b, g1_done, "h2s")
            if ti == len(TCS) - 1:
                g1_free = P.cnt["pe"]
            drain(99)

            ot_rel = {}
            wrel3 = {}
            for oc in range(NDC):
                s = oc % 2
                lw1 = wl.load("sp", hsl[s][:, :], w3s[oc, :, :],
                              waits=[("cv", cvm["w3s"])] +
                              wrel3.get(s, [g2_done] if oc < 2 else []))
                lw = xq.load("pool", xfv[s][:, :],
                             src[oc * 128:(oc + 1) * 128, t0:t0 + 1024],
                             waits=in_marks.get(tc, []) + xshare.get(s, [a_done]))
                dp = None
                p1 = None
                for hf in range(2):
                    j = oc * 2 + hf
                    bi = j % 4
                    b1 = bank(bi)
                    for ic in range(NHC):
                        p1 = P.op("pe", "matmul", b1, hsl[s][:, ic * 128:(ic + 1) * 128],
                                  a_sb[:, ic * 1024 + hf * 512: ic * 1024 + hf * 512 + 512],
                                  start=(ic == 0), stop=(ic == NHC - 1),
                                  waits=[lw1, g2_done] + bank_rel.get(bi, [])
                                  if ic == 0 else (),
                                  inc=("pe", 1) if ic == NHC - 1 else None, tag="f_w3")
                    dp = P.op("dve", "scalar_tensor_tensor", otf[hf][:, :], b1,
                              s3b[:, oc:oc + 1], xfv[s][:, hf * 512:hf * 512 + 512],
                              ALU.mult, ALU.add,
                              waits=[("pe", p1), lw] + ot_rel.get(hf, []),
                              inc=("dve", 1), tag="f_e")
                    bank_rel[bi] = [("dve", dp)]
                    sw = st.load("pool",
                                 dst[oc * 128:(oc + 1) * 128,
                                     t0 + hf * 512:t0 + hf * 512 + 512],
                                 otf[hf][:, :], waits=[("dve", dp)])
                    ot_rel[hf] = [sw]
                xshare[s] = [("dve", dp)]
                wrel3[s] = [("pe", p1)]
            marks_out[tc] = [(ld, P.cnt[ld]) for ld in st.ld if P.cnt[ld] > 0]
        return {"marks": marks_out, "lnv_free": lnv_free, "g1_free": g1_free,
                "end_dve": P.cnt["dve"]}

    def emit_attn_b(src, dst, io=None):
        import collections
        strip = A.view(0, 128, 32 * 512, BF16)
        qsb = A.view(32 * 1024, 128, NDC * 512, BF16)
        ksb = A.view(40 * 1024, 128, NDC * 512, BF16)
        vloc = A.view(48 * 1024, 128, 4 * 1024, BF16)
        kbf = [A.view(56 * 1024 + s * 8192, 128, 4096, BF16) for s in range(2)]
        vbf = [A.view(72 * 1024 + s * 8192, 128, 4096, BF16) for s in range(2)]
        retrT = A.view(88 * 1024, 128, NDC * 512, BF16)
        lnb = A.view(96 * 1024, 128, NDC * 512, BF16)
        otf = [A.view(112 * 1024 + s * 2048, 128, 512, F32) for s in range(2)]
        xf2 = [A.view(116 * 1024 + s * 2048, 128, 512, F32) for s in range(2)]
        wsl = [A.view(120 * 1024 + s * 2048, 128, 1024, BF16) for s in range(4)]
        wtb = [A.view(128 * 1024 + s * 2048, 128, 512, F32) for s in range(2)]
        tmpw = [A.view(132 * 1024 + s * 2048, 128, 512, F32) for s in range(2)]
        mbb = A.view(136 * 1024, 128, 512, F32)
        rbb = A.view(138 * 1024, 128, 512, F32)
        xfv = [A.view(140 * 1024 + s * 2048, 128, 512, F32) for s in range(2)]
        sxb = [A.view(144 * 1024 + s * 1024, 128, 512, BF16) for s in range(2)]
        sxqv = A.view(146 * 1024, 128, 512, BF16)
        smx = A.view(148 * 1024, 1, 512, F32)
        srx = A.view(150 * 1024, 1, 512, F32)
        tev = A.view(152 * 1024, 1, 1024, F32)
        wvr = A.view(156 * 1024, 128, NDC * 1024, BF16)

        io = io or {}
        in_marks = io.get("marks", {})
        lnv_gate = [("pe", io["lnv_free"])] if io.get("lnv_free") else []
        g1_gate = [("pe", io["g1_free"])] if io.get("g1_free") else []
        entry_pe = P.cnt["pe"]
        P.sem("bw")
        P.after("sp", [("cv", cvm["attnb"]), ("pe", entry_pe)])
        for ic in range(NDC):
            P.op("sp", "dma_start", out=wvr[:, ic * 1024:(ic + 1) * 1024],
                 in_=wvbs[ic * 128:(ic + 1) * 128, :], inc=("bw", 16), tag="bwv")
        BW = [("bw", P.cnt["bw"])]
        wb4 = DBuf(P, "wb4", 4)

        bg = collections.deque()

        def drain(n):
            for _ in range(min(n, len(bg))):
                bg.popleft()()

        lrel = {}
        xshare = {}
        ln_ready = {}
        pv_guard = {}

        def queue_stats4(tc):
            t0 = tc * 512
            st = {}

            def dc_step(dc):
                def f():
                    s = dc % 2
                    lw = xs.load("pool", xfv[s],
                                 src[dc * 128:(dc + 1) * 128, t0:t0 + 512],
                                 waits=in_marks.get(tc // 2, []) + lnv_gate +
                                 xshare.get(s, []))
                    ap = P.op("act", "activation", out=sxqv, in_=xfv[s],
                              func=AF.Square, waits=[lw] + lrel.get("qq", []),
                              inc=("act", 1), tag="lnsq")
                    dp = P.op("dve", "tensor_copy", sxb[s], xfv[s],
                              waits=[lw] + lrel.get(("b", s), []),
                              inc=("dve", 1), tag="lncp")
                    xshare[s] = [("act", ap), ("dve", dp)]
                    pb = P.op("pe", "matmul", bank(4)[0:1, :], onbf[:, :], sxb[s],
                              start=(dc == 0), stop=(dc == NDC - 1),
                              waits=[("dve", dp)], inc=("pe", 1), tag="ln_sx")
                    pq = P.op("pe", "matmul", bank(5)[0:1, :], onbf[:, :], sxqv,
                              start=(dc == 0), stop=(dc == NDC - 1),
                              waits=[("act", ap)], inc=("pe", 1), tag="ln_sq")
                    lrel[("b", s)] = [("pe", pb)]
                    lrel["qq"] = [("pe", pq)]
                    if dc == NDC - 1:
                        st["pelast"] = max(pb, pq)
                return f

            def math():
                pw = [("pe", st["pelast"])]
                dm = P.op("dve", "tensor_scalar_mul", smx[0:1, :], bank(4)[0:1, :],
                          1.0 / D, waits=pw, inc=("dve", 1), tag="ln_m")
                P.op("dve", "tensor_scalar_mul", tev[0:1, 0:512], bank(5)[0:1, :],
                     1.0 / D, tag="ln_e")
                a2 = P.op("act", "activation", out=tev[0:1, 512:1024],
                          in_=smx[0:1, :], func=AF.Square,
                          waits=pw + [("dve", dm)], inc=("act", 1), tag="ln_ms")
                d3 = P.op("dve", "tensor_sub", tev[0:1, 0:512], tev[0:1, 0:512],
                          tev[0:1, 512:1024], waits=[("act", a2)], inc=("dve", 1),
                          tag="ln_v")
                a3 = P.op("act", "activation", out=tev[0:1, 512:1024],
                          in_=tev[0:1, 0:512], func=AF.Sqrt, bias=cstb[0:1, 0:1],
                          waits=[("dve", d3)], inc=("act", 1), tag="ln_s")
                d4 = P.op("dve", "reciprocal", srx[0:1, :], tev[0:1, 512:1024],
                          waits=[("act", a3)], inc=("dve", 1), tag="ln_r")
                st["rdy"] = d4

            def bc():
                pm = P.op("pe", "matmul", bank(4), onf[0:1, :], smx[0:1, :],
                          start=True, stop=True,
                          waits=[("dve", st["rdy"])] + st.get("bl", []),
                          inc=("pe", 1), tag="bcm")
                P.op("dve", "tensor_copy", mbb[:, :], bank(4), waits=[("pe", pm)],
                     tag="bcmc")
                pr = P.op("pe", "matmul", bank(5), onf[0:1, :], srx[0:1, :],
                          start=True, stop=True, waits=[("dve", st["rdy"])],
                          inc=("pe", 1), tag="bcr")
                last = P.op("dve", "tensor_copy", rbb[:, :], bank(5),
                            waits=[("pe", pr)], inc=("dve", 1), tag="bcrc")
                st["done"] = last

            return [dc_step(d) for d in range(NDC)] + [math, bc], st

        def queue_norm4(tc, stt):
            t0 = tc * 512

            def step(dc):
                def f():
                    s = dc % 2
                    w = [("dve", stt["done"])] + xshare.get(s, [])
                    if dc == 0 and (tc + 1) in pv_guard:
                        w.append(("pe", pv_guard[tc + 1]))
                    lw = xl.load("pool", xfv[s],
                                 src[dc * 128:(dc + 1) * 128, t0:t0 + 512],
                                 waits=g1_gate + w)
                    P.op("dve", "tensor_sub", xfv[s][:, :], xfv[s][:, :], mbb[:, :],
                         waits=[lw] + ([("pe", pv_guard[tc + 1])]
                                       if (dc == 0 and (tc + 1) in pv_guard) else []),
                         tag="b_sb")
                    post = P.op("dve", "tensor_mul", lnb[:, dc * 512:(dc + 1) * 512],
                                xfv[s][:, :], rbb[:, :], waits=g1_gate,
                                inc=("dve", 1), tag="b_ml")
                    xshare[s] = [("dve", post)]
                    if dc == NDC - 1:
                        ln_ready[tc] = post
                return f

            return [step(d) for d in range(NDC)]

        steps, stt = queue_stats4(7)
        for f in steps:
            f()
        for f in queue_norm4(7, stt):
            f()

        retr_prev = []
        marks_b = {}
        kvst = {}
        qrel = {}
        wrel = {}
        strip_hold = []
        retr_hold = []
        ot_rel = {}
        xr_rel = {}
        kb_rel = {}
        vb_rel = {}
        wt_rel = {}
        ev = {}

        for tc in range(7, -1, -1):
            t0 = tc * 512
            if tc not in ln_ready:
                drain(99)
            last_ln = ln_ready[tc]

            qk_rel = {}
            dq = dk = None
            for mat, dstb, tag in (("q", qsb, "b_q"), ("k", ksb, "b_k")):
                wt = wqbs if mat == "q" else wkbs
                for oc in range(NDC):
                    lw1 = wb4.load("sp", wsl[wb4.i % 4][:, :], wt[oc, :, :],
                                   waits=wrel.get(wb4.i % 4, []))
                    slot = wsl[(wb4.i - 1) % 4]
                    bi = (0 if mat == "q" else 2) + oc % 2
                    bq = bank(bi)
                    pq = None
                    for ic in range(NDC):
                        pq = P.op("pe", "matmul", bq,
                                  slot[:, ic * 128:(ic + 1) * 128],
                                  lnb[:, ic * 512:(ic + 1) * 512],
                                  start=(ic == 0), stop=(ic == NDC - 1),
                                  waits=[lw1, ("dve", last_ln)] + ev.get(bi, []) +
                                  qk_rel.get((mat, oc % 2), []) if ic == 0 else (),
                                  inc=("pe", 1) if ic == NDC - 1 else None, tag=tag)
                    wrel[(wb4.i - 1) % 4] = [("pe", pq)]
                    if mat == "q":
                        dq = P.op("act", "activation",
                                  out=qsb[:, oc * 512:(oc + 1) * 512], in_=bq,
                                  func=AF.Copy,
                                  waits=[("pe", pq)] +
                                  (qrel.get("q", []) if oc == 0 else []),
                                  inc=("act", 1), tag="b_qc")
                        qk_rel[(mat, oc % 2)] = [("act", dq)]
                    else:
                        dk = P.op("dve", "tensor_copy",
                                  ksb[:, oc * 512:(oc + 1) * 512], bq,
                                  waits=[("pe", pq)] +
                                  (qrel.get("k", []) if oc == 0 else []),
                                  inc=("dve", 1), tag="b_kc")
                        qk_rel[(mat, oc % 2)] = [("dve", dk)]
                    drain(1)
            sk = qst.load("pool", kS[tc, :, :], ksb[:, :],
                          waits=[("dve", dk)])
            qrel["k"] = [sk]
            qrel["q"] = [("act", dq)]
            pv = None
            v_rel = {}
            vdone = None
            for tb in range(4):
                for hf in range(2):
                    bv = bank(6 + (tb * 2 + hf) % 2)
                    wv = BW + [("dve", last_ln)] + retr_prev + \
                        v_rel.get((tb * 2 + hf) % 2, [])
                    for ic in range(NDC):
                        pv = P.op("pe", "matmul", bv,
                                  lnb[:, ic * 512 + tb * 128: ic * 512 + tb * 128 + 128],
                                  wvr[:, ic * 1024 + hf * 512: ic * 1024 + hf * 512 + 512],
                                  start=(ic == 0), stop=(ic == NDC - 1),
                                  waits=wv if ic == 0 else (),
                                  inc=("pe", 1) if ic == NDC - 1 else None, tag="b_v")
                    vdone = P.op("dve", "tensor_copy",
                                 vloc[:, tb * 1024 + hf * 512: tb * 1024 + hf * 512 + 512],
                                 bv, waits=[("pe", pv)] +
                                 (qrel.get("v", []) if tb + hf == 0 else []),
                                 inc=("dve", 1), tag="b_vc")
                    v_rel[(tb * 2 + hf) % 2] = [("dve", vdone)]
                drain(1)
            sv = qst.load("pool", vS[tc, :, :], vloc[:, :],
                          waits=[("dve", vdone)])
            qrel["v"] = [sv]
            kvst[tc] = [sk, sv]
            pv_guard[tc] = pv

            if tc > 0:
                steps, stt = queue_stats4(tc - 1)
                bg.extend(steps)
                bg.extend(queue_norm4(tc - 1, stt))
            if BTRUNC < 2:
                drain(99)
                import os as _os2
                if int(_os2.environ.get("BDBG", "0")) and tc == 6:
                    drain(99)
                    P.barrier()
                    P.op("sp", "dma_start", out=dst[0:128, 0:4096],
                         in_=lnb[:, :], inc=("st0", 16), tag="dbg")
                    P.op("sp", "dma_start", out=dst[128:256, 0:4096],
                         in_=qsb[:, :], inc=("st0", 16), tag="dbg2")
                    P.op("sp", "dma_start", out=dst[256:384, 0:4096],
                         in_=ksb[:, :], inc=("st0", 16), tag="dbg3")
                    P.barrier()
                continue

            nsb = 32 - 4 * tc
            sd_posts = {}
            strip_done = None
            for g in range(tc, 8):
                if g == tc:
                    ksrc = ksb
                    klw = [("dve", dk)]
                else:
                    s2 = g % 2
                    lk = kl.load("sp", kbf[s2][:, :], kS[g, :, :],
                                 waits=kvst[g] + kb_rel.get(s2, []))
                    ksrc = kbf[s2]
                    klw = [lk]
                pk = None
                for jj in range(4):
                    sb = 4 * g + jj
                    pp = bank(sb % 2)
                    delta = sb * 128 - t0
                    for ic in range(NDC):
                        pk = P.op("pe", "matmul", pp,
                                  ksrc[:, ic * 512 + jj * 128: ic * 512 + jj * 128 + 128],
                                  qsb[:, ic * 512:(ic + 1) * 512],
                                  start=(ic == 0), stop=(ic == NDC - 1),
                                  waits=klw + [("act", dq)] + ev.get(sb % 2, []) +
                                  strip_hold + sd_posts.get(("b", sb % 2), [])
                                  if ic == 0 else (),
                                  inc=("pe", 1) if ic == NDC - 1 else None,
                                  tag="b_s")
                    ws = sb % 2
                    if delta >= 512:
                        wp = P.op("act", "activation", out=wtb[ws][:, :], in_=cio[:, :],
                                  func=AF.Exp, scale=lndb[:, 0:1],
                                  bias=ldt[:, delta // 128:delta // 128 + 1],
                                  waits=wt_rel.get(ws, []), inc=("act", 1), tag="b_w")
                        wpd = ("act", wp)
                    else:
                        P.op("dve", "tensor_scalar", tmpw[ws][:, :], cio[:, :],
                             float(delta - 1), lndb[:, 0:1], ALU.add, ALU.mult,
                             waits=wt_rel.get(ws, []), tag="b_w1")
                        dw2 = P.op("dve", "tensor_scalar_min", tmpw[ws][:, :],
                                   tmpw[ws][:, :], 0.0, inc=("dve", 1), tag="b_w2")
                        wp = P.op("act", "activation", out=wtb[ws][:, :],
                                  in_=tmpw[ws][:, :], func=AF.Exp,
                                  bias=cstb[:, 4:5],
                                  waits=[("dve", dw2)] + wt_rel.get(ws, []),
                                  inc=("act", 1), tag="b_w3")
                        c0 = 384 - delta
                        wpm = P.op("dve", "tensor_mul", wtb[ws][:, :], wtb[ws][:, :],
                                   t1m[:, c0:c0 + 512], waits=[("act", wp)],
                                   inc=("dve", 1), tag="b_w4")
                        wpd = ("dve", wpm)
                    sd = P.op("dve", "tensor_mul",
                              strip[:, (sb - 4 * tc) * 512:(sb - 4 * tc) * 512 + 512],
                              pp, wtb[ws][:, :],
                              waits=[("pe", pk), wpd] + strip_hold,
                              inc=("dve", 1), tag="b_p")
                    sd_posts[sb - 4 * tc] = [("dve", sd)]
                    sd_posts[("b", sb % 2)] = [("dve", sd)]
                    wt_rel[ws] = [("dve", sd)]
                    strip_done = sd
                    drain(1)
                kb_rel[g % 2] = [("pe", pk)]
            strip_hold = []
            if BTRUNC < 3:
                drain(99)
                continue

            RBK = [2, 3, 6, 7]
            retr_stop = {}
            rv = None
            import os as _os
            for half in range(int(_os.environ.get("BHALF", "2"))):
                for jj in range(nsb):
                    g = tc + jj // 4
                    if g == tc:
                        vsrc = vloc
                        vlw = [("dve", vdone)]
                        jl = jj
                    else:
                        s2 = (jj // 4) % 2
                        if jj % 4 == 0:
                            lv = vl.load("sp", vbf[s2][:, :], vS[g, :, :],
                                         waits=kvst[g] + vb_rel.get(s2, []))
                            vb_rel[("lw", s2)] = [lv]
                        vsrc = vbf[s2]
                        vlw = vb_rel[("lw", s2)]
                        jl = jj % 4
                    if half == 0:
                        w0 = list(sd_posts[jj]) + vlw
                    else:
                        w0 = (([("dve", strip_done), ("dve", rv)])
                              if jj == 0 else []) + vlw
                    if jj == 0:
                        w0 = w0 + retr_hold
                    for oi in range(4):
                        oc = half * 4 + oi
                        pr = P.op("pe", "matmul", bank(RBK[oi]),
                                  vsrc[:, jl * 1024 + oc * 128: jl * 1024 + oc * 128 + 128],
                                  strip[:, jj * 512:jj * 512 + 512],
                                  start=(jj == 0), stop=(jj == nsb - 1),
                                  waits=w0 if oi == 0 else (),
                                  inc=("pe", 1) if oi == 3 else None, tag="b_r")
                        if jj == nsb - 1:
                            retr_stop[oc] = pr
                    if jj % 4 == 3 or jj == nsb - 1:
                        vb_rel[(jj // 4) % 2] = [("pe", pr)]
                for oi in range(4):
                    oc = half * 4 + oi
                    rv = P.op("dve", "tensor_copy",
                              retrT[:, oc * 512:(oc + 1) * 512], bank(RBK[oi]),
                              waits=[("pe", retr_stop[half * 4 + 3])],
                              inc=("dve", 1), tag="b_rc")
                drain(2)
            retr_done = rv
            retr_prev = [("dve", rv)]
            strip_hold = [("pe", retr_stop[max(retr_stop)])]

            if BTRUNC < 4:
                drain(99)
                continue
            wo_last = None
            for ocO in range(NDC):
                lw1 = wb4.load("sp", wsl[wb4.i % 4][:, :], wobs[ocO, :, :],
                               waits=wrel.get(wb4.i % 4, []))
                slot = wsl[(wb4.i - 1) % 4]
                s = ocO % 2
                bnk = bank(ocO % 4)
                lw = xq.load("pool", xf2[s][:, :],
                             src[ocO * 128:(ocO + 1) * 128, t0:t0 + 512],
                             waits=in_marks.get(tc // 2, []) + g1_gate +
                             xr_rel.get(s, []))
                p1 = None
                for ic in range(NDC):
                    p1 = P.op("pe", "matmul", bnk,
                              slot[:, ic * 128:(ic + 1) * 128],
                              retrT[:, ic * 512:(ic + 1) * 512],
                              start=(ic == 0), stop=(ic == NDC - 1),
                              waits=[lw1, ("dve", retr_done)] + ev.get(ocO % 4, [])
                              if ic == 0 else (),
                              inc=("pe", 1) if ic == NDC - 1 else None, tag="b_o")
                wrel[(wb4.i - 1) % 4] = [("pe", p1)]
                dp = P.op("dve", "scalar_tensor_tensor", otf[s][:, :], bnk,
                          oscb[:, 0:1], xf2[s][:, :], ALU.mult, ALU.add,
                          waits=[("pe", p1), lw] + ot_rel.get(s, []),
                          inc=("dve", 1), tag="b_e")
                ev[ocO % 4] = [("dve", dp)]
                sw = st.load("pool", dst[ocO * 128:(ocO + 1) * 128, t0:t0 + 512],
                             otf[s][:, :], waits=[("dve", dp)])
                ot_rel[s] = [sw]
                xr_rel[s] = [("dve", dp)]
                wo_last = p1
                drain(1)
            retr_hold = [("pe", wo_last)]
            if tc % 2 == 0:
                marks_b[tc // 2] = [(ld, P.cnt[ld]) for ld in st.ld
                                    if P.cnt[ld] > 0]
        drain(99)
        return {"marks": marks_b, "end_dve": P.cnt["dve"]}

    if nsub == 5:
        emit_attn_b(xin, yout)
    elif nsub == 1:
        emit_attn_a(xin, yout)
    elif nsub == 2:
        io_a = emit_attn_a(xin, xr1)
        emit_ffn(xr1, yout, io=io_a)
    elif nsub == 3:
        io_a = emit_attn_a(xin, xr1)
        io_f = emit_ffn(xr1, xr2, io=io_a)
        emit_attn_b(xr2, yout, io=io_f)
    else:
        io_a = emit_attn_a(xin, xr1)
        io_f = emit_ffn(xr1, xr2, io=io_a)
        io_b = emit_attn_b(xr2, xr3, io=io_f)
        io_b["entry_pe_gate"] = P.cnt["pe"]
        io_b["pre_dve"] = io_b["end_dve"]
        emit_ffn(xr3, yout, io=io_b)

    fin = [(s, P.cnt[s]) for s in P.cnt if s.startswith("st") and P.cnt[s] > 0]
    P.sem("fz")
    P.op("pool", "dma_start", out=xr1[0:1, 0:1], in_=yout[0:1, 0:1],
         waits=fin, inc=("fz", 16), tag="fin")

    from contextlib import ExitStack
    with ExitStack() as es:
        sems = {name: es.enter_context(nc.semaphore(name)) for name in P.cnt}
        block = es.enter_context(nc.Block())
        block.sync(lambda e: P.replay("sp", e, sems))
        block.scalar(lambda e: P.replay("act", e, sems))
        block.vector(lambda e: P.replay("dve", e, sems))
        block.tensor(lambda e: P.replay("pe", e, sems))
        block.gpsimd(lambda e: P.replay("pool", e, sems))
    return nc



def _tile_oc(WT, nic, noc):
    return np.ascontiguousarray(
        WT.reshape(nic, 128, noc, 128).transpose(2, 1, 0, 3).reshape(noc, 128, nic * 128))


def _prep_common(inp):
    f32 = np.float32

    def bd(w):
        out = np.zeros((128, 128), f32)
        wt = np.asarray(w, f32).T
        out[0:64, 0:64] = wt
        out[64:128, 64:128] = wt
        return out

    p = np.arange(128)[:, None]
    c = np.arange(896)[None, :]
    return {
        "wqa": bd(inp["L0_attn_a_Wq"]),
        "wka": bd(inp["L0_attn_a_Wk"]),
        "wva": bd(inp["L0_attn_a_Wv"]),
        "wut": np.ascontiguousarray(np.asarray(inp["L0_attn_a_Wu"], f32).T),
        "but": np.ascontiguousarray(np.asarray(inp["L0_attn_a_bu"], f32).reshape(NDC, 128).T),
        "wqbt": _tile_oc(np.asarray(inp["L0_attn_b_Wq"], f32).T, NDC, NDC),
        "wkbt": _tile_oc(np.asarray(inp["L0_attn_b_Wk"], f32).T, NDC, NDC),
        "wvbt": np.ascontiguousarray(np.asarray(inp["L0_attn_b_Wv"], f32).T),
        "wobt": _tile_oc(np.asarray(inp["L0_attn_b_Wo"], f32).T, NDC, NDC),
        "dlog": np.asarray(inp["L0_attn_b_decay_logit"], f32).reshape(1, 1),
        "oscl": np.asarray(inp["L0_attn_b_out_scale"], f32).reshape(1, 1),
        "w1t": _tile_oc(np.asarray(inp["L0_ffn_w1_w"], f32).T, NDC, NHC),
        "w2t": _tile_oc(np.asarray(inp["L0_ffn_w2_w"], f32).T, NDC, NHC),
        "h1t": _tile_oc(np.asarray(inp["L0_ffn_h1_w"], f32).T, NHC, NHC),
        "h2t": _tile_oc(np.asarray(inp["L0_ffn_h2_w"], f32).T, NHC, NHC),
        "w3t": _tile_oc(np.asarray(inp["L0_ffn_w3_w"], f32).T, NHC, NDC),
        "s1v": np.ascontiguousarray(np.asarray(inp["L0_ffn_w1_s"], f32).reshape(NHC, 128).T),
        "s2v": np.ascontiguousarray(np.asarray(inp["L0_ffn_w2_s"], f32).reshape(NHC, 128).T),
        "sh1v": np.ascontiguousarray(np.asarray(inp["L0_ffn_h1_s"], f32).reshape(NHC, 128).T),
        "sh2v": np.ascontiguousarray(np.asarray(inp["L0_ffn_h2_s"], f32).reshape(NHC, 128).T),
        "s3v": np.ascontiguousarray(np.asarray(inp["L0_ffn_w3_s"], f32).reshape(NDC, 128).T),
        "m1c": (p > c - 384).astype(f32),
        "ioc": (np.arange(128)[:, None] - np.arange(512)[None, :]).astype(f32),
        "dm1": (128.0 * np.arange(32)[None, :] - 1.0).astype(f32),
        "on1": np.ones((1, 128), f32),
        "cstv": np.tile(np.array([[EPS, EXP_SHIFT, 0.0, SCALE_B, LNSB, 0.0]], f32),
                        (128, 1)),
    }


def kernel(**inp):
    x = np.asarray(inp["L0_attn_a_x"], np.float32)
    xT = np.ascontiguousarray(x.transpose(0, 2, 1))
    common = _prep_common(inp)
    nc = build_program(NSUB)
    in_maps = [dict(common, xT=np.ascontiguousarray(xT[b])) for b in range(NCORES)]
    res = run_bass_kernel_spmd(nc, in_maps, core_ids=list(range(NCORES)))
    y = np.stack([np.asarray(r["y"]) for r in res.results])
    return np.ascontiguousarray(y.transpose(0, 2, 1)).astype(np.float32)


# version 18 (final)
# speedup vs baseline: 1.0891x; 1.0891x over previous
import math
import numpy as np
import concourse.bass as bass
import concourse.mybir as mybir
from concourse.bass_utils import run_bass_kernel_spmd

F32 = mybir.dt.float32
BF16 = mybir.dt.bfloat16
U8 = mybir.dt.uint8
AF = mybir.ActivationFunctionType
ALU = mybir.AluOpType

NCORES = 8
T = 4096
D = 1024
NDC = D // 128
DH = 4096
NHC = DH // 128
SCALE_A = float(D) ** -0.25
SCALE_B = 1.0 / math.sqrt(float(D))
LNSB = math.log(SCALE_B)
EPS = 1e-5
EXP_SHIFT = -20.0

NSUB = 4
BTRUNC = 4

ENGS = ("sp", "act", "dve", "pe", "pool")


class Prog:

    def __init__(self):
        self.ops = {e: [] for e in ENGS}
        self.cnt = {}
        self.pending = {e: {} for e in ENGS}

    def sem(self, name):
        assert name not in self.cnt
        self.cnt[name] = 0

    def op(self, eng, method, *args, waits=(), inc=None, tag="", **kwargs):
        w = dict(self.pending[eng])
        self.pending[eng] = {}
        for s, v in waits:
            assert v <= self.cnt[s], (tag, s, v, self.cnt[s])
            if w.get(s, -1) < v:
                w[s] = v
        post = None
        if inc is not None:
            s, n = inc
            self.cnt[s] += n
            post = self.cnt[s]
        self.ops[eng].append((tuple(w.items()), method, args, kwargs, inc, tag))
        return post

    def after(self, eng, waits):
        p = self.pending[eng]
        for s, v in waits:
            if p.get(s, -1) < v:
                p[s] = v

    def barrier(self, exclude=("cv",)):
        snap = {s: v for s, v in self.cnt.items() if v > 0 and s not in exclude}
        for e in ENGS:
            p = self.pending[e]
            for s, v in snap.items():
                if p.get(s, -1) < v:
                    p[s] = v

    def replay(self, eng, proxy, sems):
        for waits, method, args, kwargs, inc, tag in self.ops[eng]:
            for s, v in waits:
                proxy.wait_ge(sems[s], v)
            inst = getattr(proxy, method)(*args, **kwargs)
            if inc is not None:
                assert inst is not None, f"op {tag} returned no instruction"
                inst.then_inc(sems[inc[0]], inc[1])


class Arena:
    def __init__(self, nc, pbytes):
        self.h = nc.alloc_sbuf_tensor("arena", [128, pbytes], U8)
        self.pbytes = pbytes
        self._v = {}

    def view(self, off, npart, cols, dtype):
        isz = mybir.dt.size(dtype)
        assert off % 64 == 0 and off + cols * isz <= self.pbytes, (off, cols, isz)
        key = (off, npart, cols, str(dtype))
        if key not in self._v:
            self._v[key] = self.h.bitcast(dtype).ap()[0:npart, off // isz: off // isz + cols]
        return self._v[key]


class DBuf:

    def __init__(self, P, name, nslots=2):
        self.P, self.name, self.n = P, name, nslots
        self.ld = [f"{name}{i}" for i in range(nslots)]
        for s in self.ld:
            P.sem(s)
        self.issued = [0] * nslots
        self.i = 0

    def load(self, qeng, dst, src, waits=()):
        s = self.i % self.n
        self.i += 1
        post = self.P.op(qeng, "dma_start", out=dst, in_=src, waits=waits,
                         inc=(self.ld[s], 16), tag=self.name)
        self.issued[s] += 1
        assert post == 16 * self.issued[s]
        return (self.ld[s], post)


def build_program(nsub=NSUB):
    nc = bass.Bass("TRN2", target_bir_lowering=False)
    P = Prog()
    for s in ("pe", "act", "dve", "pool"):
        P.sem(s)

    xin = nc.dram_tensor("xT", [D, T], F32, kind="ExternalInput").ap()
    wqa = nc.dram_tensor("wqa", [128, 128], F32, kind="ExternalInput").ap()
    wka = nc.dram_tensor("wka", [128, 128], F32, kind="ExternalInput").ap()
    wva = nc.dram_tensor("wva", [128, 128], F32, kind="ExternalInput").ap()
    wut = nc.dram_tensor("wut", [D, D], F32, kind="ExternalInput").ap()
    but = nc.dram_tensor("but", [128, NDC], F32, kind="ExternalInput").ap()
    wqbt = nc.dram_tensor("wqbt", [NDC, 128, NDC * 128], F32, kind="ExternalInput").ap()
    wkbt = nc.dram_tensor("wkbt", [NDC, 128, NDC * 128], F32, kind="ExternalInput").ap()
    wvbt = nc.dram_tensor("wvbt", [D, D], F32, kind="ExternalInput").ap()
    wobt = nc.dram_tensor("wobt", [NDC, 128, NDC * 128], F32, kind="ExternalInput").ap()
    dlog = nc.dram_tensor("dlog", [1, 1], F32, kind="ExternalInput").ap()
    oscl = nc.dram_tensor("oscl", [1, 1], F32, kind="ExternalInput").ap()
    w1t = nc.dram_tensor("w1t", [NHC, 128, NDC * 128], F32, kind="ExternalInput").ap()
    w2t = nc.dram_tensor("w2t", [NHC, 128, NDC * 128], F32, kind="ExternalInput").ap()
    h1t = nc.dram_tensor("h1t", [NHC, 128, NHC * 128], F32, kind="ExternalInput").ap()
    h2t = nc.dram_tensor("h2t", [NHC, 128, NHC * 128], F32, kind="ExternalInput").ap()
    w3t = nc.dram_tensor("w3t", [NDC, 128, NHC * 128], F32, kind="ExternalInput").ap()
    s1v = nc.dram_tensor("s1v", [128, NHC], F32, kind="ExternalInput").ap()
    s2v = nc.dram_tensor("s2v", [128, NHC], F32, kind="ExternalInput").ap()
    sh1v = nc.dram_tensor("sh1v", [128, NHC], F32, kind="ExternalInput").ap()
    sh2v = nc.dram_tensor("sh2v", [128, NHC], F32, kind="ExternalInput").ap()
    s3v = nc.dram_tensor("s3v", [128, NDC], F32, kind="ExternalInput").ap()
    m1c = nc.dram_tensor("m1c", [128, 896], F32, kind="ExternalInput").ap()
    ioc = nc.dram_tensor("ioc", [128, 512], F32, kind="ExternalInput").ap()
    dm1 = nc.dram_tensor("dm1", [1, 32], F32, kind="ExternalInput").ap()
    on1 = nc.dram_tensor("on1", [1, 128], F32, kind="ExternalInput").ap()
    cstv = nc.dram_tensor("cstv", [128, 6], F32, kind="ExternalInput").ap()
    yout = nc.dram_tensor("y", [D, T], F32, kind="ExternalOutput").ap()

    xr1 = nc.dram_tensor("xr1", [D, T], F32, kind="Internal").ap()
    xr2 = nc.dram_tensor("xr2", [D, T], F32, kind="Internal").ap()
    xr3 = nc.dram_tensor("xr3", [D, T], F32, kind="Internal").ap()
    w1s = nc.dram_tensor("w1s", [NHC, 128, NDC * 128], BF16, kind="Internal").ap()
    w2s = nc.dram_tensor("w2s", [NHC, 128, NDC * 128], BF16, kind="Internal").ap()
    h1s = nc.dram_tensor("h1s", [NHC, 128, NHC * 128], BF16, kind="Internal").ap()
    h2s = nc.dram_tensor("h2s", [NHC, 128, NHC * 128], BF16, kind="Internal").ap()
    w3s = nc.dram_tensor("w3s", [NDC, 128, NHC * 128], BF16, kind="Internal").ap()
    wqbs = nc.dram_tensor("wqbs", [NDC, 128, NDC * 128], BF16, kind="Internal").ap()
    wkbs = nc.dram_tensor("wkbs", [NDC, 128, NDC * 128], BF16, kind="Internal").ap()
    wvbs = nc.dram_tensor("wvbs", [D, D], BF16, kind="Internal").ap()
    wobs = nc.dram_tensor("wobs", [NDC, 128, NDC * 128], BF16, kind="Internal").ap()
    qS = nc.dram_tensor("qS", [8, 128, NDC * 512], BF16, kind="Internal").ap()
    kS = nc.dram_tensor("kS", [8, 128, NDC * 512], BF16, kind="Internal").ap()
    vS = nc.dram_tensor("vS", [8, 128, 4 * 1024], BF16, kind="Internal").ap()

    def sbuf(name, p, cols, dtype):
        return nc.alloc_sbuf_tensor(name, [p, cols], dtype).ap()

    t1m = sbuf("t1m", 128, 896, F32)
    t0m = sbuf("t0m", 128, 896, BF16)
    cio = sbuf("cio", 128, 512, F32)
    ldt = sbuf("ldt", 128, 32, F32)
    lndb = sbuf("lndb", 128, 1, F32)
    oscb = sbuf("oscb", 128, 1, F32)
    onbf = sbuf("onbf", 128, 1, BF16)
    onf = sbuf("onf", 1, 128, F32)
    busb = sbuf("busb", 128, NDC, F32)
    s1b = sbuf("s1b", 128, NHC, F32)
    s2b = sbuf("s2b", 128, NHC, F32)
    sh1b = sbuf("sh1b", 128, NHC, F32)
    sh2b = sbuf("sh2b", 128, NHC, F32)
    s3b = sbuf("s3b", 128, NDC, F32)
    wqab = sbuf("wqab", 128, 128, BF16)
    wkab = sbuf("wkab", 128, 128, BF16)
    wvab = sbuf("wvab", 128, 128, BF16)
    sc0 = sbuf("sc0", 1, 8, F32)
    ones2 = sbuf("ones2", 2, 64, BF16)
    dm1s = sbuf("dm1s", 1, 32, F32)
    cstb = sbuf("cstb", 128, 6, F32)

    ARENA = 196 * 1024
    A = Arena(nc, ARENA)
    ps = [nc.alloc_psum_tensor(f"ps{i}", [128, 1024], F32).ap() for i in range(4)]

    def bank(i):
        return ps[i // 2][:, (i % 2) * 512:(i % 2) * 512 + 512]

    def pair(i):
        return ps[i]

    xl = DBuf(P, "xl", 2)
    xs = DBuf(P, "xs", 2)
    xq = DBuf(P, "xq", 2)
    wl = DBuf(P, "wl", 2)
    w2l = DBuf(P, "w2l", 2)
    kl = DBuf(P, "kl", 2)
    vl = DBuf(P, "vl", 2)
    ql = DBuf(P, "ql", 1)
    st = DBuf(P, "st", 2)
    qst = DBuf(P, "qst", 2)

    P.sem("c0")
    for dst_ap, src_ap in (
        (t1m[:, :], m1c[:, :]), (cio[:, :], ioc[:, :]), (onf[:, :], on1[:, :]),
        (busb[:, :], but[:, :]), (s1b[:, :], s1v[:, :]), (s2b[:, :], s2v[:, :]),
        (sh1b[:, :], sh1v[:, :]), (sh2b[:, :], sh2v[:, :]), (s3b[:, :], s3v[:, :]),
        (sc0[0:1, 0:1], dlog[:, :]), (sc0[0:1, 1:2], oscl[:, :]),
        (dm1s[0:1, :], dm1[:, :]), (cstb[:, :], cstv[:, :]),
    ):
        P.op("sp", "dma_start", out=dst_ap, in_=src_ap, inc=("c0", 16), tag="c0")
    CV = [("c0", P.cnt["c0"])]

    P.op("dve", "memset", onbf[:, :], 1.0, tag="onbf")
    P.op("dve", "memset", ones2[:, :], 1.0, tag="one2")
    P.op("dve", "tensor_scalar", t0m[:, :], t1m[:, :], -1.0, 1.0, ALU.mult, ALU.add,
         waits=CV, tag="t0m")
    a0 = P.op("act", "activation", out=sc0[0:1, 2:3], in_=sc0[0:1, 0:1],
              func=AF.Sigmoid, waits=CV, inc=("act", 1), tag="sig")
    d0 = P.op("dve", "tensor_scalar_max", sc0[0:1, 3:4], sc0[0:1, 2:3], 1e-38,
              waits=[("act", a0)], inc=("dve", 1), tag="sigm")
    a1 = P.op("act", "activation", out=sc0[0:1, 4:5], in_=sc0[0:1, 3:4], func=AF.Ln,
              waits=[("dve", d0)], inc=("act", 1), tag="lnd")
    p1 = P.op("pe", "matmul", bank(0)[:, 0:1], onf[0:1, :], sc0[0:1, 4:5],
              start=True, stop=True, waits=[("act", a1)], inc=("pe", 1), tag="bc1")
    p2 = P.op("pe", "matmul", bank(1)[:, 0:1], onf[0:1, :], sc0[0:1, 1:2],
              start=True, stop=True, waits=CV, inc=("pe", 1), tag="bc2")
    p3 = P.op("pe", "matmul", bank(2)[:, 0:32], onf[0:1, :], dm1s[0:1, :],
              start=True, stop=True, waits=CV, inc=("pe", 1), tag="bc3")
    P.op("dve", "tensor_copy", lndb[:, :], bank(0)[:, 0:1], waits=[("pe", p1)],
         tag="lndc")
    P.op("dve", "tensor_copy", oscb[:, :], bank(1)[:, 0:1], waits=[("pe", p2)],
         tag="oscc")
    d3 = P.op("dve", "tensor_scalar", ldt[:, :], bank(2)[:, 0:32], lndb[:, 0:1],
              LNSB, ALU.mult, ALU.add, waits=[("pe", p3)], inc=("dve", 1), tag="ldt")

    wu_sb = A.view(0, 128, NDC * 1024, BF16)
    P.barrier()

    P.sem("cv")
    cvm = {}

    def cast_dma(dst_ap, src_ap, tag):
        P.op("pool", "dma_start", out=dst_ap, in_=src_ap, inc=("cv", 16), tag=tag)

    cast_dma(wqab[:, :], wqa[:, :], "cvqa")
    cast_dma(wkab[:, :], wka[:, :], "cvka")
    cast_dma(wvab[:, :], wva[:, :], "cvva")
    cvm["small"] = P.cnt["cv"]
    for ic in range(NDC):
        cast_dma(wu_sb[:, ic * 1024:(ic + 1) * 1024],
                 wut[ic * 128:(ic + 1) * 128, :], "cvwu")
    cvm["wu"] = P.cnt["cv"]
    if nsub >= 2:
        for oc in range(NHC):
            cast_dma(w1s[oc, :, :], w1t[oc, :, :], "cvw1")
        cvm["w1s"] = P.cnt["cv"]
        for oc in range(NHC):
            cast_dma(w2s[oc, :, :], w2t[oc, :, :], "cvw2")
        cvm["w2s"] = P.cnt["cv"]
        for oc in range(NHC):
            cast_dma(h1s[oc, :, :], h1t[oc, :, :], "cvh1")
        cvm["h1s"] = P.cnt["cv"]
        for oc in range(NHC):
            cast_dma(h2s[oc, :, :], h2t[oc, :, :], "cvh2")
        cvm["h2s"] = P.cnt["cv"]
        for oc in range(NDC):
            cast_dma(w3s[oc, :, :], w3t[oc, :, :], "cvw3")
        cvm["w3s"] = P.cnt["cv"]
    if nsub >= 3:
        for oc in range(NDC):
            sl = slice(oc * 128, (oc + 1) * 128)
            cast_dma(wqbs[oc, :, :], wqbt[oc, :, :], "cvqb")
            cast_dma(wkbs[oc, :, :], wkbt[oc, :, :], "cvkb")
            cast_dma(wvbs[sl, :], wvbt[sl, :], "cvvb")
            cast_dma(wobs[oc, :, :], wobt[oc, :, :], "cvob")
        cvm["attnb"] = P.cnt["cv"]

    def ln_stats(src, t0, tcols, sm, sr, xfv, xbv, xqv, tev, rel):
        nh = tcols // 512
        assert nh in (1, 2)
        for dc in range(NDC):
            s = dc % 2
            lw = xs.load("pool", xfv[s], src[dc * 128:(dc + 1) * 128, t0:t0 + tcols],
                         waits=rel.get(("f", s), []))
            ap = P.op("act", "activation", out=xqv[s], in_=xfv[s], func=AF.Square,
                      waits=[lw] + rel.get(("q", s), []), inc=("act", 1), tag="lnsq")
            dp = P.op("dve", "tensor_copy", xbv[s], xfv[s],
                      waits=[lw] + rel.get(("b", s), []), inc=("dve", 1), tag="lncp")
            rel[("f", s)] = [("act", ap), ("dve", dp)]
            pb = pq = None
            for h in range(nh):
                c = slice(h * 512, (h + 1) * 512)
                pb = P.op("pe", "matmul", pair(2)[0:1, c], onbf[:, :], xbv[s][:, c],
                          start=(dc == 0), stop=(dc == NDC - 1), waits=[("dve", dp)],
                          inc=("pe", 1), tag="ln_sx")
                pq = P.op("pe", "matmul", pair(3)[0:1, c], onbf[:, :], xqv[s][:, c],
                          start=(dc == 0), stop=(dc == NDC - 1), waits=[("act", ap)],
                          inc=("pe", 1), tag="ln_sq")
            rel[("b", s)] = [("pe", pb)]
            rel[("q", s)] = [("pe", pq)]
        pw = [("pe", P.cnt["pe"])]
        dm = P.op("dve", "tensor_scalar_mul", sm, pair(2)[0:1, 0:tcols], 1.0 / D,
                  waits=pw, inc=("dve", 1), tag="ln_m")
        P.op("dve", "tensor_scalar_mul", tev[0:1, 0:tcols], pair(3)[0:1, 0:tcols],
             1.0 / D, tag="ln_e")
        a2 = P.op("act", "activation", out=tev[0:1, tcols:2 * tcols], in_=sm,
                  func=AF.Square, waits=pw + [("dve", dm)], inc=("act", 1),
                  tag="ln_ms")
        d3 = P.op("dve", "tensor_sub", tev[0:1, 0:tcols], tev[0:1, 0:tcols],
                  tev[0:1, tcols:2 * tcols], waits=[("act", a2)], inc=("dve", 1),
                  tag="ln_v")
        a3 = P.op("act", "activation", out=tev[0:1, tcols:2 * tcols],
                  in_=tev[0:1, 0:tcols], func=AF.Sqrt, bias=cstb[0:1, 0:1],
                  waits=[("dve", d3)], inc=("act", 1), tag="ln_s")
        d4 = P.op("dve", "reciprocal", sr, tev[0:1, tcols:2 * tcols],
                  waits=[("act", a3)], inc=("dve", 1), tag="ln_r")
        return d4

    def ln_bcast(sm, sr, mb, rb, tcols, ready):
        nh = tcols // 512
        last = None
        for h in range(nh):
            c = slice(h * 512, (h + 1) * 512)
            w = [("dve", ready)] + ([("dve", last)] if last else [])
            pm = P.op("pe", "matmul", bank(4), onf[0:1, :], sm[0:1, c],
                      start=True, stop=True, waits=w, inc=("pe", 1), tag="bcm")
            P.op("dve", "tensor_copy", mb[:, c], bank(4), waits=[("pe", pm)],
                 tag="bcmc")
            pr = P.op("pe", "matmul", bank(5), onf[0:1, :], sr[0:1, c],
                      start=True, stop=True, waits=[("dve", ready)], inc=("pe", 1),
                      tag="bcr")
            last = P.op("dve", "tensor_copy", rb[:, c], bank(5), waits=[("pe", pr)],
                        inc=("dve", 1), tag="bcrc")
        return last

    def emit_attn_a(src, dst):
        import collections
        mb = A.view(16 * 1024, 128, T, F32)
        rb = A.view(32 * 1024, 128, T, F32)
        yT = A.view(48 * 1024, 128, NDC * T, BF16)
        ln_dc = A.view(112 * 1024, 128, T, BF16)
        qT = A.view(120 * 1024, 128, T, BF16)
        kT = A.view(128 * 1024, 128, T, BF16)
        v2 = A.view(136 * 1024, 128, 32 * 192, BF16)
        v2r = v2.rearrange("p (a b) -> p a b", b=192)
        xfh = [A.view(148 * 1024 + s * 8 * 1024, 128, 2048, F32) for s in range(2)]
        prb = [A.view(169 * 1024 + s * 2048, 128, 1024, BF16) for s in range(6)]
        tmpf = A.view(183 * 1024, 128, 512, F32)
        rbsF = A.view(185 * 1024, 64, 1024, F32)
        tev = A.view(183 * 1024, 1, 1024, F32)
        sxq = [A.view(161 * 1024 + s * 1024, 128, 512, BF16) for s in range(4)]
        sxb = [A.view(165 * 1024 + s * 1024, 128, 512, BF16) for s in range(4)]
        sxf = [A.view(169 * 1024 + s * 2048, 128, 512, F32) for s in range(4)]
        smcr = A.view(189 * 1024, 1, 1024, F32)
        smc = smcr[0:1, 0:512]
        src_c = smcr[0:1, 512:1024]

        lrel = {}
        smcr2 = [smc, src_c]

        def stats_steps(tc):
            t0s = tc * 512
            par = tc % 2
            bx, bq2 = bank(4 + 2 * par), bank(5 + 2 * par)
            stt = {}
            out = []

            def dc_step(dcs):
                def f():
                    s = dcs % 4
                    lw = xs.load("sp", sxf[s],
                                 src[dcs * 128:(dcs + 1) * 128, t0s:t0s + 512],
                                 waits=lrel.get(("f", s), []))
                    ap = P.op("act", "activation", out=sxq[s], in_=sxf[s],
                              func=AF.Square, waits=[lw] + lrel.get(("q", s), []),
                              inc=("act", 1), tag="lnsq")
                    dp = P.op("dve", "tensor_copy", sxb[s], sxf[s],
                              waits=[lw] + lrel.get(("b", s), []),
                              inc=("dve", 1), tag="lncp")
                    lrel[("f", s)] = [("act", ap), ("dve", dp)]
                    pb = P.op("pe", "matmul", bx[0:1, :], onbf[:, :], sxb[s],
                              start=(dcs == 0), stop=(dcs == NDC - 1),
                              waits=[("dve", dp)], inc=("pe", 1), tag="ln_sx")
                    pq = P.op("pe", "matmul", bq2[0:1, :], onbf[:, :], sxq[s],
                              start=(dcs == 0), stop=(dcs == NDC - 1),
                              waits=[("act", ap)], inc=("pe", 1), tag="ln_sq")
                    lrel[("b", s)] = [("pe", pb)]
                    lrel[("q", s)] = [("pe", pq)]
                    if dcs == NDC - 1:
                        stt["pelast"] = max(pb, pq)
                return f

            def math():
                pw = [("pe", stt["pelast"])] + lrel.get(("m", par), []) + \
                    lrel.get("bcg", [])
                dm = P.op("dve", "tensor_scalar_mul", smcr2[0][0:1, :], bx[0:1, :],
                          1.0 / D, waits=pw, inc=("dve", 1), tag="ln_m")
                P.op("dve", "tensor_scalar_mul", tev[0:1, 0:512], bq2[0:1, :],
                     1.0 / D, tag="ln_e")
                a2 = P.op("act", "activation", out=tev[0:1, 512:1024],
                          in_=smcr2[0][0:1, :], func=AF.Square,
                          waits=pw + [("dve", dm)], inc=("act", 1), tag="ln_ms")
                d3 = P.op("dve", "tensor_sub", tev[0:1, 0:512], tev[0:1, 0:512],
                          tev[0:1, 512:1024], waits=[("act", a2)], inc=("dve", 1),
                          tag="ln_v")
                a3 = P.op("act", "activation", out=tev[0:1, 512:1024],
                          in_=tev[0:1, 0:512], func=AF.Sqrt, bias=cstb[0:1, 0:1],
                          waits=[("dve", d3)], inc=("act", 1), tag="ln_s")
                d4 = P.op("dve", "reciprocal", smcr2[1][0:1, :], tev[0:1, 512:1024],
                          waits=[("act", a3)], inc=("dve", 1), tag="ln_r")
                stt["rdy"] = d4

            def bc():
                c = slice(t0s, t0s + 512)
                pm = P.op("pe", "matmul", bx, onf[0:1, :], smcr2[0][0:1, :],
                          start=True, stop=True, waits=[("dve", stt["rdy"])],
                          inc=("pe", 1), tag="bcm")
                P.op("dve", "tensor_copy", mb[:, c], bx, waits=[("pe", pm)],
                     tag="bcmc")
                pr = P.op("pe", "matmul", bq2, onf[0:1, :], smcr2[1][0:1, :],
                          start=True, stop=True, waits=[("dve", stt["rdy"])],
                          inc=("pe", 1), tag="bcr")
                last = P.op("dve", "tensor_copy", rb[:, c], bq2, waits=[("pe", pr)],
                            inc=("dve", 1), tag="bcrc")
                lrel[("m", par)] = [("dve", last)]
                lrel["bcg"] = [("pe", pr)]
            return [dc_step(d) for d in range(NDC)], math, bc

        prev = None
        for tc in range(8):
            steps, mth, bcf = stats_steps(tc)
            for j, f in enumerate(steps):
                f()
                if prev is not None:
                    if j == 1:
                        prev[0]()
                    elif j == 3:
                        prev[1]()
            prev = (mth, bcf)
        prev[0]()
        prev[1]()
        P.barrier()

        v2o = P.op("dve", "memset", v2r[:, :, 64:128], 1.0, inc=("dve", 1), tag="v2o")
        P.after("pe", [("dve", v2o)])

        xrel = {}
        ln_guard = [0]

        def emit_norm_chunk(dc, tc):
            h = tc // 4
            if tc % 4 == 0:
                w = xrel.get(h, []) + ([("pe", ln_guard[0])] if tc == 0 else [])
                xrel[("lw", h)] = xl.load(
                    "sp", xfh[h], src[dc * 128:(dc + 1) * 128,
                                      h * 2048:(h + 1) * 2048], waits=w)
            c = slice(tc * 512, tc * 512 + 512)
            cc = slice((tc % 4) * 512, (tc % 4) * 512 + 512)
            w = [xrel[("lw", h)]]
            if tc == 0:
                w.append(("pe", ln_guard[0]))
            P.op("dve", "tensor_sub", tmpf[:, :], xfh[h][:, cc], mb[:, c],
                 waits=w, tag="a_sb")
            post = P.op("dve", "tensor_mul", ln_dc[:, c], tmpf[:, :], rb[:, c],
                        inc=("dve", 1), tag="a_ml")
            xrel[h] = [("dve", post)]
            return post

        ln_done = None
        for tc in range(8):
            ln_done = emit_norm_chunk(0, tc)
        ln_ready = {0: ln_done}

        halfslot_rel = {}
        slot_rel = {}
        ybank_rel = []
        abc_rel = []
        tail_prev = {}
        tailq = collections.deque()
        P.after("pe", [("cv", cvm["wu"])])

        def drain_tail(n):
            for _ in range(min(n, len(tailq))):
                tailq.popleft()()

        for dc in range(NDC):
            last_ln = ln_ready[dc]
            qkv_rel = {}
            item = 0
            qk_done = last_ln
            v_done = last_ln
            pv = None
            for tc in range(8):
                c = slice(tc * 512, tc * 512 + 512)
                r = item % 4
                item += 1
                pq = P.op("pe", "matmul", bank(r), wqab[:, :], ln_dc[:, c],
                          start=True, stop=True,
                          waits=[("dve", last_ln)] + qkv_rel.get(r, []),
                          inc=("pe", 1), tag="a_q")
                dq = P.op("dve", "tensor_copy", qT[:, c], bank(r),
                          waits=[("pe", pq)], inc=("dve", 1), tag="a_qc")
                qkv_rel[r] = [("dve", dq)]
                r = item % 4
                item += 1
                pk = P.op("pe", "matmul", bank(r), wkab[:, :], ln_dc[:, c],
                          start=True, stop=True, waits=qkv_rel.get(r, []),
                          inc=("pe", 1), tag="a_k")
                dk = P.op("dve", "tensor_copy", kT[:, c], bank(r), waits=[("pe", pk)],
                          inc=("dve", 1), tag="a_kc")
                qkv_rel[r] = [("dve", dk)]
                qk_done = dk
                r = item % 4
                item += 1
                wv = qkv_rel.get(r, [])
                for t4 in range(4):
                    tb = tc * 4 + t4
                    pv = P.op("pe", "matmul", bank(r)[:, t4 * 128:t4 * 128 + 128],
                              ln_dc[:, tb * 128:tb * 128 + 128], wvab[:, :],
                              start=True, stop=True, waits=wv if t4 == 0 else (),
                              inc=("pe", 1), tag="a_v")
                src_v = bank(r).rearrange("p (a cc dd) -> p a cc dd", a=4, dd=64)
                P.op("dve", "tensor_copy",
                     v2r[:, tc * 4:tc * 4 + 4, 0:64], src_v[:, :, 0, :],
                     waits=[("pe", pv)], tag="a_vcA")
                dv = P.op("dve", "tensor_copy",
                          v2r[:, tc * 4:tc * 4 + 4, 128:192], src_v[:, :, 1, :],
                          inc=("dve", 1), tag="a_vcB")
                qkv_rel[r] = [("dve", dv)]
                v_done = dv
                drain_tail(2)
            drain_tail(8)
            ln_guard[0] = pv

            for tc in range(8):
                nkb = 4 * tc + 4
                qc = slice(tc * 512, tc * 512 + 512)
                yb = ps[3]
                seq = list(range(4 * tc, nkb)) + list(range(0, 4 * tc))
                probs = {}
                ypost = {}
                sA = {}
                sB = {}

                def emit_scores(j, kb):
                    kc = slice(kb * 128, kb * 128 + 128)
                    sa, sb_ = (2 * j) % 6, (2 * j + 1) % 6
                    wv = [("dve", max(qk_done, v_done))] + halfslot_rel.get(sa, [])
                    sA[j] = P.op("pe", "matmul", bank(sa), kT[0:64, kc],
                                 qT[0:64, qc], start=True, stop=True, waits=wv,
                                 inc=("pe", 1), tag="a_sA")
                    sB[j] = P.op("pe", "matmul", bank(sb_), kT[64:128, kc],
                                 qT[64:128, qc], start=True, stop=True,
                                 waits=halfslot_rel.get(sb_, []), inc=("pe", 1),
                                 tag="a_sB")

                def emit_probs(j, kb):
                    slot = prb[j % 6]
                    pp = ps[j % 3]
                    aa = P.op("act", "activation", out=slot[:, :], in_=pp[:, :],
                              func=AF.Exp, scale=cstb[:, 3:4], bias=cstb[:, 1:2],
                              waits=[("pe", sB[j])] + slot_rel.get(j % 6, []),
                              inc=("act", 1), tag="a_ex")
                    halfslot_rel[(2 * j) % 6] = [("act", aa)]
                    halfslot_rel[(2 * j + 1) % 6] = [("act", aa)]
                    if kb >= 4 * tc:
                        c0 = 384 - (kb * 128 - tc * 512)
                        P.op("dve", "tensor_mul", slot[:, 0:512], slot[:, 0:512],
                             t0m[:, c0:c0 + 512], waits=[("act", aa)], tag="a_mA")
                        mm = P.op("dve", "tensor_mul", slot[:, 512:1024],
                                  slot[:, 512:1024], t0m[:, c0:c0 + 512],
                                  inc=("dve", 1), tag="a_mB")
                        probs[j] = ([("dve", mm)], [("dve", mm)])
                    else:
                        probs[j] = ([("act", aa)], [("act", aa)])

                def emit_y(j, kb):
                    slot = prb[j % 6]
                    wA = list(probs[j][0])
                    wB = list(probs[j][1])
                    if j == 0:
                        wA = wA + ybank_rel
                    P.op("pe", "matmul", yb[:, 0:512],
                         v2[:, kb * 192:kb * 192 + 128], slot[:, 0:512],
                         start=(j == 0), stop=(j == nkb - 1), waits=wA, tag="a_yA")
                    ypost[j] = P.op("pe", "matmul", yb[:, 512:1024],
                                    v2[:, kb * 192 + 64:kb * 192 + 192],
                                    slot[:, 512:1024], start=(j == 0),
                                    stop=(j == nkb - 1), waits=wB, inc=("pe", 1),
                                    tag="a_yB")
                    slot_rel[j % 6] = [("pe", ypost[j])]

                for j, kb in enumerate(seq):
                    emit_scores(j, kb)
                    emit_probs(j, kb)
                    drain_tail(2)
                    if j == 2:
                        if dc + 1 < NDC:
                            p = emit_norm_chunk(dc + 1, tc)
                            if tc == 7:
                                ln_ready[dc + 1] = p
                    if j >= 3:
                        emit_y(j - 3, seq[j - 3])
                for j in (nkb - 3, nkb - 2, nkb - 1):
                    emit_y(j, seq[j])

                def make_tail(dc=dc, tc=tc, qc=qc, yb=yb, ystop=ypost[nkb - 1]):
                    st_ = {}

                    def f_rcp():
                        P.op("dve", "reciprocal", rbsF[0:64, 0:512],
                             yb[64:128, 0:512],
                             waits=[("pe", ystop)] + abc_rel, tag="a_rcA")
                        st_["rcp"] = P.op("dve", "reciprocal",
                                          rbsF[0:64, 512:1024],
                                          yb[0:64, 512:1024], inc=("dve", 1),
                                          tag="a_rcB")

                    def f_dtyA():
                        st_["dtyA"] = P.op("dve", "tensor_mul",
                                           yT[:, dc * T:dc * T + T][0:64, qc],
                                           yb[0:64, 0:512], rbsF[0:64, 0:512],
                                           inc=("dve", 1), tag="a_yeA")

                    def f_dtyB():
                        post = P.op("dve", "tensor_mul",
                                    yT[:, dc * T:dc * T + T][64:128, qc],
                                    yb[64:128, 512:1024], rbsF[0:64, 512:1024],
                                    inc=("dve", 1), tag="a_yeB")
                        abc_rel[:] = [("dve", post)]
                        ybank_rel[:] = [("dve", st_["dtyA"]), ("dve", post)]

                    return [f_rcp, f_dtyA, f_dtyB]

                tailq.extend(make_tail())
                drain_tail(2)
        drain_tail(99)
        atail_dve = P.cnt["dve"]

        P.barrier()
        otf = [A.view(145 * 1024 + s * 2048, 128, 512, F32) for s in range(4)]
        xrf = [A.view(153 * 1024 + s * 2048, 128, 512, F32) for s in range(4)]
        xw = DBuf(P, "xw", 4)
        st4 = DBuf(P, "st4", 4)
        ev = {}
        ot_rel = {}
        xr_rel = {}
        marks_a = {}
        i = 0
        for tc in range(7, -1, -1):
            for dcO in range(NDC):
                qc = slice(tc * 512, tc * 512 + 512)
                s = i % 4
                bnk = bank(i % 4)
                p1 = None
                for ic in range(NDC):
                    p1 = P.op("pe", "matmul", bnk,
                              wu_sb[:, ic * 1024 + dcO * 128: ic * 1024 + dcO * 128 + 128],
                              yT[:, ic * T:ic * T + T][:, qc],
                              start=(ic == 0), stop=(ic == NDC - 1),
                              waits=ev.get(i % 4, []) if ic == 0 else (),
                              inc=("pe", 1) if ic == NDC - 1 else None, tag="a_wu")
                lw = xw.load("sp", xrf[s], src[dcO * 128:(dcO + 1) * 128, qc],
                             waits=xr_rel.get(s, []))
                dp = P.op("dve", "scalar_tensor_tensor", otf[s][:, :], bnk,
                          busb[:, dcO:dcO + 1], xrf[s][:, :], ALU.add, ALU.add,
                          waits=[("pe", p1), lw] + ot_rel.get(s, []),
                          inc=("dve", 1), tag="a_e")
                ev[i % 4] = [("dve", dp)]
                sw = st4.load("sp", dst[dcO * 128:(dcO + 1) * 128, qc],
                              otf[s][:, :], waits=[("dve", dp)])
                ot_rel[s] = [sw]
                xr_rel[s] = [("dve", dp)]
                i += 1
            if tc % 2 == 0:
                marks_a[tc // 2] = [(ld, P.cnt[ld]) for ld in st4.ld
                                    if P.cnt[ld] > 0]
        return {"marks": marks_a, "pre_dve": atail_dve,
                "end_dve": P.cnt["dve"]}

    def emit_ffn(src, dst, io=None):
        import collections
        a_sb = A.view(0, 128, NHC * 1024, BF16)
        g1 = A.view(64 * 1024, 128, NHC * 1024, BF16)
        lnv = A.view(128 * 1024, 128, NDC * 1024, BF16)
        smx = A.view(128 * 1024, 1, 1024, F32)
        srx = A.view(132 * 1024, 1, 1024, F32)
        tev = A.view(136 * 1024, 1, 2048, F32)
        mbb = A.view(144 * 1024, 128, 1024, F32)
        rbb = A.view(148 * 1024, 128, 1024, F32)
        w1sl = [A.view(152 * 1024 + s * 2048, 128, 1024, BF16) for s in range(2)]
        w2sl = [A.view(156 * 1024 + s * 2048, 128, 1024, BF16) for s in range(2)]
        hsl = [A.view(160 * 1024 + s * 8192, 128, 4096, BF16) for s in range(2)]
        siv = [A.view(176 * 1024 + s * 1024, 128, 512, BF16) for s in range(2)]
        xfv = [A.view(178 * 1024 + s * 4096, 128, 1024, F32) for s in range(2)]
        sxb = [A.view(186 * 1024 + s * 2048, 128, 1024, BF16) for s in range(2)]
        sxqv = A.view(190 * 1024, 128, 1024, BF16)
        otf = [A.view(192 * 1024 + s * 2048, 128, 512, F32) for s in range(2)]

        bg = collections.deque()
        io = io or {}
        in_marks = io.get("marks", {})
        w_gate = [("dve", io["end_dve"])] if "end_dve" in io else []
        entry_pe = P.cnt["pe"]
        if "pre_dve" in io:
            P.after("pe", [("dve", io["pre_dve"])])
        if "entry_pe_gate" in io:
            w_gate = w_gate + [("pe", io["entry_pe_gate"])]

        def drain(n):
            for _ in range(min(n, len(bg))):
                bg.popleft()()

        lrel = {}
        xshare = {}
        ln_ready = {}
        bank_rel = {}

        def queue_stats(tc):
            t0 = tc * 1024
            st = {}

            def dc_step(dc):
                def f():
                    s = dc % 2
                    lw = xs.load("pool", xfv[s],
                                 src[dc * 128:(dc + 1) * 128, t0:t0 + 1024],
                                 waits=in_marks.get(tc, []) + xshare.get(s, []))
                    ap = P.op("act", "activation", out=sxqv, in_=xfv[s],
                              func=AF.Square, waits=[lw] + lrel.get("qq", []),
                              inc=("act", 1), tag="lnsq")
                    dp = P.op("dve", "tensor_copy", sxb[s], xfv[s],
                              waits=[lw] + lrel.get(("b", s), []),
                              inc=("dve", 1), tag="lncp")
                    xshare[s] = [("act", ap), ("dve", dp)]
                    pb = pq = None
                    for h in range(2):
                        c = slice(h * 512, (h + 1) * 512)
                        pb = P.op("pe", "matmul", pair(2)[0:1, c], onbf[:, :],
                                  sxb[s][:, c], start=(dc == 0), stop=(dc == NDC - 1),
                                  waits=[("dve", dp)], inc=("pe", 1), tag="ln_sx")
                        pq = P.op("pe", "matmul", pair(3)[0:1, c], onbf[:, :],
                                  sxqv[:, c], start=(dc == 0), stop=(dc == NDC - 1),
                                  waits=[("act", ap)], inc=("pe", 1), tag="ln_sq")
                    lrel[("b", s)] = [("pe", pb)]
                    lrel["qq"] = [("pe", pq)]
                    if dc == NDC - 1:
                        st["pelast"] = max(pb, pq)
                return f

            def math():
                pw = [("pe", st["pelast"])]
                dm = P.op("dve", "tensor_scalar_mul", smx[0:1, :], pair(2)[0:1, 0:1024],
                          1.0 / D, waits=pw, inc=("dve", 1), tag="ln_m")
                P.op("dve", "tensor_scalar_mul", tev[0:1, 0:1024],
                     pair(3)[0:1, 0:1024], 1.0 / D, tag="ln_e")
                a2 = P.op("act", "activation", out=tev[0:1, 1024:2048],
                          in_=smx[0:1, :], func=AF.Square,
                          waits=pw + [("dve", dm)], inc=("act", 1), tag="ln_ms")
                d3 = P.op("dve", "tensor_sub", tev[0:1, 0:1024], tev[0:1, 0:1024],
                          tev[0:1, 1024:2048], waits=[("act", a2)], inc=("dve", 1),
                          tag="ln_v")
                a3 = P.op("act", "activation", out=tev[0:1, 1024:2048],
                          in_=tev[0:1, 0:1024], func=AF.Sqrt, bias=cstb[0:1, 0:1],
                          waits=[("dve", d3)], inc=("act", 1), tag="ln_s")
                d4 = P.op("dve", "reciprocal", srx[0:1, :], tev[0:1, 1024:2048],
                          waits=[("act", a3)], inc=("dve", 1), tag="ln_r")
                st["rdy"] = d4

            def bc(h):
                def f():
                    c = slice(h * 512, (h + 1) * 512)
                    pm = P.op("pe", "matmul", bank(4), onf[0:1, :], smx[0:1, c],
                              start=True, stop=True,
                              waits=[("dve", st["rdy"])] + st.get("bl", []),
                              inc=("pe", 1), tag="bcm")
                    P.op("dve", "tensor_copy", mbb[:, c], bank(4),
                         waits=[("pe", pm)], tag="bcmc")
                    pr = P.op("pe", "matmul", bank(5), onf[0:1, :], srx[0:1, c],
                              start=True, stop=True, waits=[("dve", st["rdy"])],
                              inc=("pe", 1), tag="bcr")
                    last = P.op("dve", "tensor_copy", rbb[:, c], bank(5),
                                waits=[("pe", pr)], inc=("dve", 1), tag="bcrc")
                    st["bl"] = [("dve", last)]
                    st["done"] = last
                return f

            return [dc_step(d) for d in range(NDC)] + [math, bc(0), bc(1)], st

        def queue_norm(tc, wfree, stt):
            t0 = tc * 1024

            def step(dc):
                def f():
                    s = dc % 2
                    lw = xl.load("pool", xfv[s],
                                 src[dc * 128:(dc + 1) * 128, t0:t0 + 1024],
                                 waits=[("dve", stt["done"])] + xshare.get(s, []))
                    P.op("dve", "tensor_sub", xfv[s][:, :], xfv[s][:, :], mbb[:, :],
                         waits=[lw] + ([("pe", wfree)] if dc == 0 else []), tag="f_sb")
                    post = P.op("dve", "tensor_mul", lnv[:, dc * 1024:(dc + 1) * 1024],
                                xfv[s][:, :], rbb[:, :], inc=("dve", 1), tag="f_ml")
                    xshare[s] = [("dve", post)]
                    if dc == NDC - 1:
                        ln_ready[tc] = post
                return f

            return [step(d) for d in range(NDC)]

        TCS = [3, 2, 1, 0]
        steps, stt = queue_stats(TCS[0])
        for f in steps:
            f()
        for f in queue_norm(TCS[0], 0, stt):
            f()

        marks_out = {}
        lnv_free = g1_free = None
        for ti, tc in enumerate(TCS):
            t0 = tc * 1024
            drain(99) if tc not in ln_ready else None
            last_ln = ln_ready[tc]
            wrel = {}
            p1 = p2 = None
            for oc in range(NHC):
                s = oc % 2
                lw1 = wl.load("sp", w1sl[s][:, :], w1s[oc, :, :],
                              waits=[("cv", cvm["w1s"])] + w_gate +
                              wrel.get(("1", s), []))
                lw2 = w2l.load("sp", w2sl[s][:, :], w2s[oc, :, :],
                               waits=[("cv", cvm["w2s"])] + w_gate +
                               wrel.get(("2", s), []))
                for hf in range(2):
                    j = oc * 2 + hf
                    b1i, b2i = j % 2, 2 + j % 2
                    b1, b2 = bank(b1i), bank(b2i)
                    for ic in range(NDC):
                        p1 = P.op("pe", "matmul", b1, w1sl[s][:, ic * 128:(ic + 1) * 128],
                                  lnv[:, ic * 1024 + hf * 512: ic * 1024 + hf * 512 + 512],
                                  start=(ic == 0), stop=(ic == NDC - 1),
                                  waits=[lw1, ("dve", last_ln)] + bank_rel.get(b1i, [])
                                  if ic == 0 else (),
                                  inc=("pe", 1) if ic == NDC - 1 else None, tag="f_w1")
                    sip = P.op("act", "activation", out=siv[hf][:, :], in_=b1,
                               func=AF.Silu, scale=s1b[:, oc:oc + 1],
                               waits=[("pe", p1)] + bank_rel.get(("si", hf), []),
                               inc=("act", 1), tag="f_si")
                    bank_rel[b1i] = [("act", sip)]
                    for ic in range(NDC):
                        p2 = P.op("pe", "matmul", b2, w2sl[s][:, ic * 128:(ic + 1) * 128],
                                  lnv[:, ic * 1024 + hf * 512: ic * 1024 + hf * 512 + 512],
                                  start=(ic == 0), stop=(ic == NDC - 1),
                                  waits=[lw2] + bank_rel.get(b2i, []) if ic == 0 else (),
                                  inc=("pe", 1) if ic == NDC - 1 else None, tag="f_w2")
                    gp = P.op("dve", "scalar_tensor_tensor",
                              a_sb[:, oc * 1024 + hf * 512: oc * 1024 + hf * 512 + 512],
                              b2, s2b[:, oc:oc + 1], siv[hf][:, :], ALU.mult, ALU.mult,
                              waits=[("pe", p2), ("act", sip)], inc=("dve", 1),
                              tag="f_ga")
                    bank_rel[b2i] = [("dve", gp)]
                    bank_rel[("si", hf)] = [("dve", gp)]
                wrel[("1", s)] = [("pe", p1)]
                wrel[("2", s)] = [("pe", p2)]
            a_done = ("dve", P.cnt["dve"])
            w2_free = p2
            if ti == len(TCS) - 1:
                lnv_free = P.cnt["pe"]

            if ti + 1 < len(TCS):
                steps, stt = queue_stats(TCS[ti + 1])
                bg.extend(steps)
                norm_steps = queue_norm(TCS[ti + 1], w2_free, stt)
            else:
                norm_steps = []

            def h_layer(wsrc, rsrc, wdst, sv, ready, cvk):
                wrel2 = {}
                gp = None
                p1 = None
                for oc in range(NHC):
                    s = oc % 2
                    lw1 = wl.load("sp", hsl[s][:, :], wsrc[oc, :, :],
                                  waits=[("cv", cvm[cvk])] +
                                  wrel2.get(s, [ready] if oc < 2 else []))
                    for hf in range(2):
                        j = oc * 2 + hf
                        bi = j % 4
                        b1 = bank(bi)
                        for ic in range(NHC):
                            p1 = P.op("pe", "matmul", b1,
                                      hsl[s][:, ic * 128:(ic + 1) * 128],
                                      rsrc[:, ic * 1024 + hf * 512: ic * 1024 + hf * 512 + 512],
                                      start=(ic == 0), stop=(ic == NHC - 1),
                                      waits=[lw1, ready] + bank_rel.get(bi, [])
                                      if ic == 0 else (),
                                      inc=("pe", 1) if ic == NHC - 1 else None,
                                      tag="f_h")
                        gp = P.op("act", "activation",
                                  out=wdst[:, oc * 1024 + hf * 512: oc * 1024 + hf * 512 + 512],
                                  in_=b1, func=AF.Gelu, scale=sv[:, oc:oc + 1],
                                  waits=[("pe", p1)], inc=("act", 1), tag="f_ge")
                        bank_rel[bi] = [("act", gp)]
                    wrel2[s] = [("pe", p1)]
                    drain(1)
                return ("act", gp)

            g1_done = h_layer(h1s, a_sb, g1, sh1b, a_done, "h1s")
            bg.extend(norm_steps)
            g2_done = h_layer(h2s, g1, a_sb, sh2b, g1_done, "h2s")
            if ti == len(TCS) - 1:
                g1_free = P.cnt["pe"]
            drain(99)

            ot_rel = {}
            wrel3 = {}
            for oc in range(NDC):
                s = oc % 2
                lw1 = wl.load("sp", hsl[s][:, :], w3s[oc, :, :],
                              waits=[("cv", cvm["w3s"])] +
                              wrel3.get(s, [g2_done] if oc < 2 else []))
                lw = xq.load("pool", xfv[s][:, :],
                             src[oc * 128:(oc + 1) * 128, t0:t0 + 1024],
                             waits=in_marks.get(tc, []) + xshare.get(s, [a_done]))
                dp = None
                p1 = None
                for hf in range(2):
                    j = oc * 2 + hf
                    bi = j % 4
                    b1 = bank(bi)
                    for ic in range(NHC):
                        p1 = P.op("pe", "matmul", b1, hsl[s][:, ic * 128:(ic + 1) * 128],
                                  a_sb[:, ic * 1024 + hf * 512: ic * 1024 + hf * 512 + 512],
                                  start=(ic == 0), stop=(ic == NHC - 1),
                                  waits=[lw1, g2_done] + bank_rel.get(bi, [])
                                  if ic == 0 else (),
                                  inc=("pe", 1) if ic == NHC - 1 else None, tag="f_w3")
                    dp = P.op("dve", "scalar_tensor_tensor", otf[hf][:, :], b1,
                              s3b[:, oc:oc + 1], xfv[s][:, hf * 512:hf * 512 + 512],
                              ALU.mult, ALU.add,
                              waits=[("pe", p1), lw] + ot_rel.get(hf, []),
                              inc=("dve", 1), tag="f_e")
                    bank_rel[bi] = [("dve", dp)]
                    sw = st.load("pool",
                                 dst[oc * 128:(oc + 1) * 128,
                                     t0 + hf * 512:t0 + hf * 512 + 512],
                                 otf[hf][:, :], waits=[("dve", dp)])
                    ot_rel[hf] = [sw]
                xshare[s] = [("dve", dp)]
                wrel3[s] = [("pe", p1)]
            marks_out[tc] = [(ld, P.cnt[ld]) for ld in st.ld if P.cnt[ld] > 0]
        return {"marks": marks_out, "lnv_free": lnv_free, "g1_free": g1_free,
                "end_dve": P.cnt["dve"]}

    def emit_attn_b(src, dst, io=None):
        import collections
        strip = A.view(0, 128, 32 * 512, BF16)
        qsb = A.view(32 * 1024, 128, NDC * 512, BF16)
        ksb = A.view(40 * 1024, 128, NDC * 512, BF16)
        vloc = A.view(48 * 1024, 128, 4 * 1024, BF16)
        kbf = [A.view(56 * 1024 + s * 8192, 128, 4096, BF16) for s in range(2)]
        vbf = [A.view(72 * 1024 + s * 8192, 128, 4096, BF16) for s in range(2)]
        retrT = A.view(88 * 1024, 128, NDC * 512, BF16)
        lnb = A.view(96 * 1024, 128, NDC * 512, BF16)
        otf = [A.view(112 * 1024 + s * 2048, 128, 512, F32) for s in range(2)]
        xf2 = [A.view(116 * 1024 + s * 2048, 128, 512, F32) for s in range(2)]
        wsl = [A.view(120 * 1024 + s * 2048, 128, 1024, BF16) for s in range(4)]
        wtb = [A.view(128 * 1024 + s * 2048, 128, 512, F32) for s in range(2)]
        tmpw = [A.view(132 * 1024 + s * 2048, 128, 512, F32) for s in range(2)]
        mbb = A.view(136 * 1024, 128, 512, F32)
        rbb = A.view(138 * 1024, 128, 512, F32)
        xfv = [A.view(140 * 1024 + s * 2048, 128, 512, F32) for s in range(2)]
        sxb = [A.view(144 * 1024 + s * 1024, 128, 512, BF16) for s in range(2)]
        sxqv = A.view(146 * 1024, 128, 512, BF16)
        smx = A.view(148 * 1024, 1, 512, F32)
        srx = A.view(150 * 1024, 1, 512, F32)
        tev = A.view(152 * 1024, 1, 1024, F32)
        wvr = A.view(156 * 1024, 128, NDC * 1024, BF16)

        io = io or {}
        in_marks = io.get("marks", {})
        lnv_gate = [("pe", io["lnv_free"])] if io.get("lnv_free") else []
        g1_gate = [("pe", io["g1_free"])] if io.get("g1_free") else []
        entry_pe = P.cnt["pe"]
        P.sem("bw")
        P.after("sp", [("cv", cvm["attnb"]), ("pe", entry_pe)])
        for ic in range(NDC):
            P.op("sp", "dma_start", out=wvr[:, ic * 1024:(ic + 1) * 1024],
                 in_=wvbs[ic * 128:(ic + 1) * 128, :], inc=("bw", 16), tag="bwv")
        BW = [("bw", P.cnt["bw"])]
        wb4 = DBuf(P, "wb4", 4)

        bg = collections.deque()

        def drain(n):
            for _ in range(min(n, len(bg))):
                bg.popleft()()

        lrel = {}
        xshare = {}
        ln_ready = {}
        pv_guard = {}

        def queue_stats4(tc):
            t0 = tc * 512
            st = {}

            def dc_step(dc):
                def f():
                    s = dc % 2
                    lw = xs.load("pool", xfv[s],
                                 src[dc * 128:(dc + 1) * 128, t0:t0 + 512],
                                 waits=in_marks.get(tc // 2, []) + lnv_gate +
                                 xshare.get(s, []))
                    ap = P.op("act", "activation", out=sxqv, in_=xfv[s],
                              func=AF.Square, waits=[lw] + lrel.get("qq", []),
                              inc=("act", 1), tag="lnsq")
                    dp = P.op("dve", "tensor_copy", sxb[s], xfv[s],
                              waits=[lw] + lrel.get(("b", s), []),
                              inc=("dve", 1), tag="lncp")
                    xshare[s] = [("act", ap), ("dve", dp)]
                    pb = P.op("pe", "matmul", bank(4)[0:1, :], onbf[:, :], sxb[s],
                              start=(dc == 0), stop=(dc == NDC - 1),
                              waits=[("dve", dp)], inc=("pe", 1), tag="ln_sx")
                    pq = P.op("pe", "matmul", bank(5)[0:1, :], onbf[:, :], sxqv,
                              start=(dc == 0), stop=(dc == NDC - 1),
                              waits=[("act", ap)], inc=("pe", 1), tag="ln_sq")
                    lrel[("b", s)] = [("pe", pb)]
                    lrel["qq"] = [("pe", pq)]
                    if dc == NDC - 1:
                        st["pelast"] = max(pb, pq)
                return f

            def math():
                pw = [("pe", st["pelast"])]
                dm = P.op("dve", "tensor_scalar_mul", smx[0:1, :], bank(4)[0:1, :],
                          1.0 / D, waits=pw, inc=("dve", 1), tag="ln_m")
                P.op("dve", "tensor_scalar_mul", tev[0:1, 0:512], bank(5)[0:1, :],
                     1.0 / D, tag="ln_e")
                a2 = P.op("act", "activation", out=tev[0:1, 512:1024],
                          in_=smx[0:1, :], func=AF.Square,
                          waits=pw + [("dve", dm)], inc=("act", 1), tag="ln_ms")
                d3 = P.op("dve", "tensor_sub", tev[0:1, 0:512], tev[0:1, 0:512],
                          tev[0:1, 512:1024], waits=[("act", a2)], inc=("dve", 1),
                          tag="ln_v")
                a3 = P.op("act", "activation", out=tev[0:1, 512:1024],
                          in_=tev[0:1, 0:512], func=AF.Sqrt, bias=cstb[0:1, 0:1],
                          waits=[("dve", d3)], inc=("act", 1), tag="ln_s")
                d4 = P.op("dve", "reciprocal", srx[0:1, :], tev[0:1, 512:1024],
                          waits=[("act", a3)], inc=("dve", 1), tag="ln_r")
                st["rdy"] = d4

            def bc():
                pm = P.op("pe", "matmul", bank(4), onf[0:1, :], smx[0:1, :],
                          start=True, stop=True,
                          waits=[("dve", st["rdy"])] + st.get("bl", []),
                          inc=("pe", 1), tag="bcm")
                P.op("dve", "tensor_copy", mbb[:, :], bank(4), waits=[("pe", pm)],
                     tag="bcmc")
                pr = P.op("pe", "matmul", bank(5), onf[0:1, :], srx[0:1, :],
                          start=True, stop=True, waits=[("dve", st["rdy"])],
                          inc=("pe", 1), tag="bcr")
                last = P.op("dve", "tensor_copy", rbb[:, :], bank(5),
                            waits=[("pe", pr)], inc=("dve", 1), tag="bcrc")
                st["done"] = last

            return [dc_step(d) for d in range(NDC)] + [math, bc], st

        def queue_norm4(tc, stt):
            t0 = tc * 512

            def step(dc):
                def f():
                    s = dc % 2
                    w = [("dve", stt["done"])] + xshare.get(s, [])
                    if dc == 0 and (tc + 1) in pv_guard:
                        w.append(("pe", pv_guard[tc + 1]))
                    lw = xl.load("pool", xfv[s],
                                 src[dc * 128:(dc + 1) * 128, t0:t0 + 512],
                                 waits=g1_gate + w)
                    P.op("dve", "tensor_sub", xfv[s][:, :], xfv[s][:, :], mbb[:, :],
                         waits=[lw] + ([("pe", pv_guard[tc + 1])]
                                       if (dc == 0 and (tc + 1) in pv_guard) else []),
                         tag="b_sb")
                    post = P.op("dve", "tensor_mul", lnb[:, dc * 512:(dc + 1) * 512],
                                xfv[s][:, :], rbb[:, :], waits=g1_gate,
                                inc=("dve", 1), tag="b_ml")
                    xshare[s] = [("dve", post)]
                    if dc == NDC - 1:
                        ln_ready[tc] = post
                return f

            return [step(d) for d in range(NDC)]

        steps, stt = queue_stats4(7)
        for f in steps:
            f()
        for f in queue_norm4(7, stt):
            f()

        retr_prev = []
        marks_b = {}
        kvst = {}
        qrel = {}
        wrel = {}
        strip_hold = []
        retr_hold = []
        ot_rel = {}
        xr_rel = {}
        kb_rel = {}
        vb_rel = {}
        wt_rel = {}
        ev = {}

        for tc in range(7, -1, -1):
            t0 = tc * 512
            if tc not in ln_ready:
                drain(99)
            last_ln = ln_ready[tc]

            qk_rel = {}
            dq = dk = None
            for mat, dstb, tag in (("q", qsb, "b_q"), ("k", ksb, "b_k")):
                wt = wqbs if mat == "q" else wkbs
                for oc in range(NDC):
                    lw1 = wb4.load("sp", wsl[wb4.i % 4][:, :], wt[oc, :, :],
                                   waits=wrel.get(wb4.i % 4, []))
                    slot = wsl[(wb4.i - 1) % 4]
                    bi = (0 if mat == "q" else 2) + oc % 2
                    bq = bank(bi)
                    pq = None
                    for ic in range(NDC):
                        pq = P.op("pe", "matmul", bq,
                                  slot[:, ic * 128:(ic + 1) * 128],
                                  lnb[:, ic * 512:(ic + 1) * 512],
                                  start=(ic == 0), stop=(ic == NDC - 1),
                                  waits=[lw1, ("dve", last_ln)] + ev.get(bi, []) +
                                  qk_rel.get((mat, oc % 2), []) if ic == 0 else (),
                                  inc=("pe", 1) if ic == NDC - 1 else None, tag=tag)
                    wrel[(wb4.i - 1) % 4] = [("pe", pq)]
                    if mat == "q":
                        dq = P.op("act", "activation",
                                  out=qsb[:, oc * 512:(oc + 1) * 512], in_=bq,
                                  func=AF.Copy,
                                  waits=[("pe", pq)] +
                                  (qrel.get("q", []) if oc == 0 else []),
                                  inc=("act", 1), tag="b_qc")
                        qk_rel[(mat, oc % 2)] = [("act", dq)]
                    else:
                        dk = P.op("dve", "tensor_copy",
                                  ksb[:, oc * 512:(oc + 1) * 512], bq,
                                  waits=[("pe", pq)] +
                                  (qrel.get("k", []) if oc == 0 else []),
                                  inc=("dve", 1), tag="b_kc")
                        qk_rel[(mat, oc % 2)] = [("dve", dk)]
                    drain(1)
            sk = qst.load("pool", kS[tc, :, :], ksb[:, :],
                          waits=[("dve", dk)])
            qrel["k"] = [sk]
            qrel["q"] = [("act", dq)]
            pv = None
            v_rel = {}
            vdone = None
            for tb in range(4):
                for hf in range(2):
                    bv = bank(6 + (tb * 2 + hf) % 2)
                    wv = BW + [("dve", last_ln)] + retr_prev + \
                        v_rel.get((tb * 2 + hf) % 2, [])
                    for ic in range(NDC):
                        pv = P.op("pe", "matmul", bv,
                                  lnb[:, ic * 512 + tb * 128: ic * 512 + tb * 128 + 128],
                                  wvr[:, ic * 1024 + hf * 512: ic * 1024 + hf * 512 + 512],
                                  start=(ic == 0), stop=(ic == NDC - 1),
                                  waits=wv if ic == 0 else (),
                                  inc=("pe", 1) if ic == NDC - 1 else None, tag="b_v")
                    vdone = P.op("dve", "tensor_copy",
                                 vloc[:, tb * 1024 + hf * 512: tb * 1024 + hf * 512 + 512],
                                 bv, waits=[("pe", pv)] +
                                 (qrel.get("v", []) if tb + hf == 0 else []),
                                 inc=("dve", 1), tag="b_vc")
                    v_rel[(tb * 2 + hf) % 2] = [("dve", vdone)]
                drain(1)
            sv = qst.load("pool", vS[tc, :, :], vloc[:, :],
                          waits=[("dve", vdone)])
            qrel["v"] = [sv]
            kvst[tc] = [sk, sv]
            pv_guard[tc] = pv

            if tc > 0:
                steps, stt = queue_stats4(tc - 1)
                bg.extend(steps)
                bg.extend(queue_norm4(tc - 1, stt))
            if BTRUNC < 2:
                drain(99)
                import os as _os2
                if int(_os2.environ.get("BDBG", "0")) and tc == 6:
                    drain(99)
                    P.barrier()
                    P.op("sp", "dma_start", out=dst[0:128, 0:4096],
                         in_=lnb[:, :], inc=("st0", 16), tag="dbg")
                    P.op("sp", "dma_start", out=dst[128:256, 0:4096],
                         in_=qsb[:, :], inc=("st0", 16), tag="dbg2")
                    P.op("sp", "dma_start", out=dst[256:384, 0:4096],
                         in_=ksb[:, :], inc=("st0", 16), tag="dbg3")
                    P.barrier()
                continue

            nsb = 32 - 4 * tc
            sd_posts = {}
            strip_done = None
            for g in range(tc, 8):
                if g == tc:
                    ksrc = ksb
                    klw = [("dve", dk)]
                else:
                    s2 = g % 2
                    lk = kl.load("sp", kbf[s2][:, :], kS[g, :, :],
                                 waits=kvst[g] + kb_rel.get(s2, []))
                    ksrc = kbf[s2]
                    klw = [lk]
                pk = None
                for jj in range(4):
                    sb = 4 * g + jj
                    pp = bank(sb % 2)
                    delta = sb * 128 - t0
                    for ic in range(NDC):
                        pk = P.op("pe", "matmul", pp,
                                  ksrc[:, ic * 512 + jj * 128: ic * 512 + jj * 128 + 128],
                                  qsb[:, ic * 512:(ic + 1) * 512],
                                  start=(ic == 0), stop=(ic == NDC - 1),
                                  waits=klw + [("act", dq)] + ev.get(sb % 2, []) +
                                  strip_hold + sd_posts.get(("b", sb % 2), [])
                                  if ic == 0 else (),
                                  inc=("pe", 1) if ic == NDC - 1 else None,
                                  tag="b_s")
                    ws = sb % 2
                    if delta >= 512:
                        wp = P.op("act", "activation", out=wtb[ws][:, :], in_=cio[:, :],
                                  func=AF.Exp, scale=lndb[:, 0:1],
                                  bias=ldt[:, delta // 128:delta // 128 + 1],
                                  waits=wt_rel.get(ws, []), inc=("act", 1), tag="b_w")
                        wpd = ("act", wp)
                    else:
                        P.op("dve", "tensor_scalar", tmpw[ws][:, :], cio[:, :],
                             float(delta - 1), lndb[:, 0:1], ALU.add, ALU.mult,
                             waits=wt_rel.get(ws, []), tag="b_w1")
                        dw2 = P.op("dve", "tensor_scalar_min", tmpw[ws][:, :],
                                   tmpw[ws][:, :], 0.0, inc=("dve", 1), tag="b_w2")
                        wp = P.op("act", "activation", out=wtb[ws][:, :],
                                  in_=tmpw[ws][:, :], func=AF.Exp,
                                  bias=cstb[:, 4:5],
                                  waits=[("dve", dw2)] + wt_rel.get(ws, []),
                                  inc=("act", 1), tag="b_w3")
                        c0 = 384 - delta
                        wpm = P.op("dve", "tensor_mul", wtb[ws][:, :], wtb[ws][:, :],
                                   t1m[:, c0:c0 + 512], waits=[("act", wp)],
                                   inc=("dve", 1), tag="b_w4")
                        wpd = ("dve", wpm)
                    sd = P.op("dve", "tensor_mul",
                              strip[:, (sb - 4 * tc) * 512:(sb - 4 * tc) * 512 + 512],
                              pp, wtb[ws][:, :],
                              waits=[("pe", pk), wpd] + strip_hold,
                              inc=("dve", 1), tag="b_p")
                    sd_posts[sb - 4 * tc] = [("dve", sd)]
                    sd_posts[("b", sb % 2)] = [("dve", sd)]
                    wt_rel[ws] = [("dve", sd)]
                    strip_done = sd
                    drain(1)
                kb_rel[g % 2] = [("pe", pk)]
            strip_hold = []
            if BTRUNC < 3:
                drain(99)
                continue

            RBK = [2, 3, 6, 7]
            retr_stop = {}
            rv = None
            import os as _os
            for half in range(int(_os.environ.get("BHALF", "2"))):
                for jj in range(nsb):
                    g = tc + jj // 4
                    if g == tc:
                        vsrc = vloc
                        vlw = [("dve", vdone)]
                        jl = jj
                    else:
                        s2 = (jj // 4) % 2
                        if jj % 4 == 0:
                            lv = vl.load("sp", vbf[s2][:, :], vS[g, :, :],
                                         waits=kvst[g] + vb_rel.get(s2, []))
                            vb_rel[("lw", s2)] = [lv]
                        vsrc = vbf[s2]
                        vlw = vb_rel[("lw", s2)]
                        jl = jj % 4
                    if half == 0:
                        w0 = list(sd_posts[jj]) + vlw
                    else:
                        w0 = (([("dve", strip_done), ("dve", rv)])
                              if jj == 0 else []) + vlw
                    if jj == 0:
                        w0 = w0 + retr_hold
                    for oi in range(4):
                        oc = half * 4 + oi
                        pr = P.op("pe", "matmul", bank(RBK[oi]),
                                  vsrc[:, jl * 1024 + oc * 128: jl * 1024 + oc * 128 + 128],
                                  strip[:, jj * 512:jj * 512 + 512],
                                  start=(jj == 0), stop=(jj == nsb - 1),
                                  waits=w0 if oi == 0 else (),
                                  inc=("pe", 1) if oi == 3 else None, tag="b_r")
                        if jj == nsb - 1:
                            retr_stop[oc] = pr
                    if jj % 4 == 3 or jj == nsb - 1:
                        vb_rel[(jj // 4) % 2] = [("pe", pr)]
                for oi in range(4):
                    oc = half * 4 + oi
                    rv = P.op("dve", "tensor_copy",
                              retrT[:, oc * 512:(oc + 1) * 512], bank(RBK[oi]),
                              waits=[("pe", retr_stop[half * 4 + 3])],
                              inc=("dve", 1), tag="b_rc")
                drain(2)
            retr_done = rv
            retr_prev = [("dve", rv)]
            strip_hold = [("pe", retr_stop[max(retr_stop)])]

            if BTRUNC < 4:
                drain(99)
                continue
            wo_last = None
            for ocO in range(NDC):
                lw1 = wb4.load("sp", wsl[wb4.i % 4][:, :], wobs[ocO, :, :],
                               waits=wrel.get(wb4.i % 4, []))
                slot = wsl[(wb4.i - 1) % 4]
                s = ocO % 2
                bnk = bank(ocO % 4)
                lw = xq.load("pool", xf2[s][:, :],
                             src[ocO * 128:(ocO + 1) * 128, t0:t0 + 512],
                             waits=in_marks.get(tc // 2, []) + g1_gate +
                             xr_rel.get(s, []))
                p1 = None
                for ic in range(NDC):
                    p1 = P.op("pe", "matmul", bnk,
                              slot[:, ic * 128:(ic + 1) * 128],
                              retrT[:, ic * 512:(ic + 1) * 512],
                              start=(ic == 0), stop=(ic == NDC - 1),
                              waits=[lw1, ("dve", retr_done)] + ev.get(ocO % 4, [])
                              if ic == 0 else (),
                              inc=("pe", 1) if ic == NDC - 1 else None, tag="b_o")
                wrel[(wb4.i - 1) % 4] = [("pe", p1)]
                dp = P.op("dve", "scalar_tensor_tensor", otf[s][:, :], bnk,
                          oscb[:, 0:1], xf2[s][:, :], ALU.mult, ALU.add,
                          waits=[("pe", p1), lw] + ot_rel.get(s, []),
                          inc=("dve", 1), tag="b_e")
                ev[ocO % 4] = [("dve", dp)]
                sw = st.load("pool", dst[ocO * 128:(ocO + 1) * 128, t0:t0 + 512],
                             otf[s][:, :], waits=[("dve", dp)])
                ot_rel[s] = [sw]
                xr_rel[s] = [("dve", dp)]
                wo_last = p1
                drain(1)
            retr_hold = [("pe", wo_last)]
            if tc % 2 == 0:
                marks_b[tc // 2] = [(ld, P.cnt[ld]) for ld in st.ld
                                    if P.cnt[ld] > 0]
        drain(99)
        return {"marks": marks_b, "end_dve": P.cnt["dve"]}

    if nsub == 5:
        emit_attn_b(xin, yout)
    elif nsub == 1:
        emit_attn_a(xin, yout)
    elif nsub == 2:
        io_a = emit_attn_a(xin, xr1)
        emit_ffn(xr1, yout, io=io_a)
    elif nsub == 3:
        io_a = emit_attn_a(xin, xr1)
        io_f = emit_ffn(xr1, xr2, io=io_a)
        emit_attn_b(xr2, yout, io=io_f)
    else:
        io_a = emit_attn_a(xin, xr1)
        io_f = emit_ffn(xr1, xr2, io=io_a)
        io_b = emit_attn_b(xr2, xr3, io=io_f)
        io_b["entry_pe_gate"] = P.cnt["pe"]
        io_b["pre_dve"] = io_b["end_dve"]
        emit_ffn(xr3, yout, io=io_b)

    fin = [(s, P.cnt[s]) for s in P.cnt if s.startswith("st") and P.cnt[s] > 0]
    P.sem("fz")
    P.op("pool", "dma_start", out=xr1[0:1, 0:1], in_=yout[0:1, 0:1],
         waits=fin, inc=("fz", 16), tag="fin")

    from contextlib import ExitStack
    with ExitStack() as es:
        sems = {name: es.enter_context(nc.semaphore(name)) for name in P.cnt}
        block = es.enter_context(nc.Block())
        block.sync(lambda e: P.replay("sp", e, sems))
        block.scalar(lambda e: P.replay("act", e, sems))
        block.vector(lambda e: P.replay("dve", e, sems))
        block.tensor(lambda e: P.replay("pe", e, sems))
        block.gpsimd(lambda e: P.replay("pool", e, sems))
    return nc



def _tile_oc(WT, nic, noc):
    return np.ascontiguousarray(
        WT.reshape(nic, 128, noc, 128).transpose(2, 1, 0, 3).reshape(noc, 128, nic * 128))


def _prep_common(inp):
    f32 = np.float32

    def bd(w):
        out = np.zeros((128, 128), f32)
        wt = np.asarray(w, f32).T
        out[0:64, 0:64] = wt
        out[64:128, 64:128] = wt
        return out

    p = np.arange(128)[:, None]
    c = np.arange(896)[None, :]
    return {
        "wqa": bd(inp["L0_attn_a_Wq"]),
        "wka": bd(inp["L0_attn_a_Wk"]),
        "wva": bd(inp["L0_attn_a_Wv"]),
        "wut": np.ascontiguousarray(np.asarray(inp["L0_attn_a_Wu"], f32).T),
        "but": np.ascontiguousarray(np.asarray(inp["L0_attn_a_bu"], f32).reshape(NDC, 128).T),
        "wqbt": _tile_oc(np.asarray(inp["L0_attn_b_Wq"], f32).T, NDC, NDC),
        "wkbt": _tile_oc(np.asarray(inp["L0_attn_b_Wk"], f32).T, NDC, NDC),
        "wvbt": np.ascontiguousarray(np.asarray(inp["L0_attn_b_Wv"], f32).T),
        "wobt": _tile_oc(np.asarray(inp["L0_attn_b_Wo"], f32).T, NDC, NDC),
        "dlog": np.asarray(inp["L0_attn_b_decay_logit"], f32).reshape(1, 1),
        "oscl": np.asarray(inp["L0_attn_b_out_scale"], f32).reshape(1, 1),
        "w1t": _tile_oc(np.asarray(inp["L0_ffn_w1_w"], f32).T, NDC, NHC),
        "w2t": _tile_oc(np.asarray(inp["L0_ffn_w2_w"], f32).T, NDC, NHC),
        "h1t": _tile_oc(np.asarray(inp["L0_ffn_h1_w"], f32).T, NHC, NHC),
        "h2t": _tile_oc(np.asarray(inp["L0_ffn_h2_w"], f32).T, NHC, NHC),
        "w3t": _tile_oc(np.asarray(inp["L0_ffn_w3_w"], f32).T, NHC, NDC),
        "s1v": np.ascontiguousarray(np.asarray(inp["L0_ffn_w1_s"], f32).reshape(NHC, 128).T),
        "s2v": np.ascontiguousarray(np.asarray(inp["L0_ffn_w2_s"], f32).reshape(NHC, 128).T),
        "sh1v": np.ascontiguousarray(np.asarray(inp["L0_ffn_h1_s"], f32).reshape(NHC, 128).T),
        "sh2v": np.ascontiguousarray(np.asarray(inp["L0_ffn_h2_s"], f32).reshape(NHC, 128).T),
        "s3v": np.ascontiguousarray(np.asarray(inp["L0_ffn_w3_s"], f32).reshape(NDC, 128).T),
        "m1c": (p > c - 384).astype(f32),
        "ioc": (np.arange(128)[:, None] - np.arange(512)[None, :]).astype(f32),
        "dm1": (128.0 * np.arange(32)[None, :] - 1.0).astype(f32),
        "on1": np.ones((1, 128), f32),
        "cstv": np.tile(np.array([[EPS, EXP_SHIFT, 0.0, SCALE_B, LNSB, 0.0]], f32),
                        (128, 1)),
    }


def kernel(**inp):
    x = np.asarray(inp["L0_attn_a_x"], np.float32)
    xT = np.ascontiguousarray(x.transpose(0, 2, 1))
    common = _prep_common(inp)
    nc = build_program(NSUB)
    in_maps = [dict(common, xT=np.ascontiguousarray(xT[b])) for b in range(NCORES)]
    res = run_bass_kernel_spmd(nc, in_maps, core_ids=list(range(NCORES)))
    y = np.stack([np.asarray(r["y"]) for r in res.results])
    return np.ascontiguousarray(y.transpose(0, 2, 1)).astype(np.float32)
